# version 45
# baseline (speedup 1.0000x reference)
import numpy as np
import concourse.bass as bass
import concourse.mybir as mybir
from concourse.bass_utils import run_bass_kernel_spmd

F32 = mybir.dt.float32
AF = mybir.ActivationFunctionType
OP = mybir.AluOpType
AX = mybir.AxisListType.X

NCORES = 8
N = 2048
D = 1024
H = 16
NT, T = 4, 512
NST = 16
NC = 8
LN_EPS = 1e-5
LN256EPS = float(np.log(256.0 * 1e-6))


class Chan:
    __slots__ = ("sem", "n", "name")

    def __init__(self, name, sem):
        self.name, self.sem, self.n = name, sem, 0

    def inc(self, k=1):
        self.n += k
        return self.n


def build_nc():
    nc = bass.Bass("TRN2", target_bir_lowering=False)

    x_h = nc.dram_tensor("x", [N, D], F32, kind="ExternalInput").ap()
    wqkv_h = nc.dram_tensor("wqkv", [D, 3 * D], F32, kind="ExternalInput").ap()
    wout_h = nc.dram_tensor("wout", [D, D], F32, kind="ExternalInput").ap()
    pm2_h = nc.dram_tensor("pm2", [128, H * 128], F32, kind="ExternalInput").ap()
    wffnT_h = nc.dram_tensor("wffnT", [D, D], F32, kind="ExternalInput").ap()
    gatewT_h = nc.dram_tensor("gatewT", [D, 8], F32, kind="ExternalInput").ap()
    ut_h = nc.dram_tensor("ut", [D, 128], F32, kind="ExternalInput").ap()
    vflat_h = nc.dram_tensor("vflat", [128, D], F32, kind="ExternalInput").ap()
    y_h = nc.dram_tensor("y", [N, D], F32, kind="ExternalOutput").ap()
    qs_h = nc.dram_tensor("q_scratch", [NC, 128, N], F32, kind="Internal")
    vs_h = nc.dram_tensor("v_scratch", [N, D], F32, kind="Internal").ap()

    ident_np = np.eye(128, dtype=np.float32)
    d2h_np = np.zeros((128, 128), dtype=np.float32)
    for c in range(NC):
        for p in range(128):
            d2h_np[p, c * 16 + (2 * c + p // 64)] = 1.0
    rgate_np = np.zeros((8, 128), dtype=np.float32)
    for er in range(128):
        rgate_np[er // 16, er] = 1.0 / 16.0
    epair_np = np.zeros((16, 8 * 128), dtype=np.float32)
    for c in range(8):
        for m in range(128):
            epair_np[2 * c + m // 64, c * 128 + m] = 1.0
    cst_vals = [0.0, 1.0, LN_EPS, LN256EPS, 0.125, -0.125, 0.5, -0.5, -1.0]
    cst_np = np.tile(np.array(cst_vals, np.float32)[None, :], (128, 1))

    plan = []

    def SP(fn): plan.append(("sync", fn))
    def GP(fn): plan.append(("gpsimd", fn))
    def TE(fn): plan.append(("tensor", fn))

    def VE(fn): plan.append(("vector", fn))
    def SE(fn): plan.append(("scalar", fn))

    from contextlib import ExitStack
    st_ctx = ExitStack()

    def sbt(name, shape, dtype=F32):
        return st_ctx.enter_context(nc.sbuf_tensor(name, shape, dtype))

    def pst(name, shape, dtype=F32):
        return st_ctx.enter_context(nc.psum_tensor(name, shape, dtype))

    chans = {}

    def C(name):
        if name not in chans:
            sem = st_ctx.enter_context(nc.semaphore(name))
            chans[name] = Chan(name, sem)
        return chans[name]

    with st_ctx:
        poolA = sbt("poolA", [128, 16640])
        poolB = sbt("poolB", [128, 16384])
        poolC = sbt("poolC", [128, 8192])
        pm2 = sbt("pm2s", [128, H * 128])
        kvA = sbt("kvA", [128, H * 130])
        bpad = sbt("bpad", [128, 512])
        gk = sbt("gk", [128, NST * 16])
        ident = sbt("ident", [128, 128])
        d2h = sbt("d2h", [128, 128])
        rgate = sbt("rgate", [8, 128])
        epair = sbt("epair", [16, 8 * 128])
        cst = sbt("cst", [128, len(cst_vals)])
        stats12s = [sbt(f"stats12{i}", [128, 12]) for i in range(3)]
        mvs = [sbt(f"mv{i}", [128, 2]) for i in range(3)]
        lnvs = [sbt(f"lnv{i}", [128, 1]) for i in range(3)]
        rsbs = [sbt(f"rsb{i}", [128, 1]) for i in range(3)]
        nbs = [sbt(f"nb{i}", [128, 1]) for i in range(3)]
        smax = sbt("smax", [128, 1])
        stabk_raw = sbt("stabk_raw", [128, 16])
        stk_t = sbt("stk_t", [16, 2])
        gtmp = sbt("gtmp", [128, 32])
        gneg = sbt("gneg", [128, 4])
        gsum = sbt("gsum", [128, 4])

        ident_t = nc.inline_tensor(ident_np, "ident_c")
        d2h_t = nc.inline_tensor(d2h_np, "d2h_c")
        rgate_t = nc.inline_tensor(rgate_np, "rgate_c")
        epair_t = nc.inline_tensor(epair_np, "epair_c")
        cst_t = nc.inline_tensor(cst_np, "cst_c")
        for i, v in enumerate(cst_vals):
            nc.const_aps.register(F32, float(v), cst[:, i:i + 1])

        pb = [pst(f"pb{i}", [128, 512]) for i in range(8)]

        bar = C("bar")
        nbar = [0]

        def barrier():
            nbar[0] += 1
            tgt = 5 * nbar[0]
            bar.n = tgt
            for em, dr in ((SP, False), (TE, True), (VE, True), (SE, True),
                           (GP, False)):
                def _f(eng, tgt=tgt, dr=dr):
                    if dr:
                        eng.drain().then_inc(bar.sem, 1)
                    else:
                        eng.nop().then_inc(bar.sem, 1)
                    eng.wait_ge(bar.sem, tgt)
                em(_f)

        cMVs = [C("cMV0"), C("cMV1"), C("cMV2")]
        cRSs = [C("cRS0"), C("cRS1"), C("cRS2")]
        cNBs = [C("cNB0"), C("cNB1"), C("cNB2")]
        ln_i = [0]

        def ln_stats_chain(src_ap):
            s = ln_i[0] % 3
            ln_i[0] += 1
            cMVc, cRSc, cNBc = cMVs[s], cRSs[s], cNBs[s]
            st12, mvv, lnvv, rsv, nbv = (stats12s[s], mvs[s], lnvs[s],
                                         rsbs[s], nbs[s])
            prior_rs = cRSc.n
            vMV = cMVc.inc()
            def _f(vector, src_ap=src_ap, prior_rs=prior_rs, st12=st12,
                   mvv=mvv, cRSc=cRSc, cMVc=cMVc):
                if prior_rs:
                    vector.wait_ge(cRSc.sem, prior_rs)
                vector.bn_stats(st12[:, 0:6], src_ap[:, 0:512])
                vector.drain()
                vector.bn_stats(st12[:, 6:12], src_ap[:, 512:1024])
                vector.drain()
                vector.bn_aggr(mvv[:], st12[:]).then_inc(cMVc.sem, 1)
            VE(_f)
            vRS = cRSc.inc()
            def _f(scalar, vMV=vMV, mvv=mvv, lnvv=lnvv, rsv=rsv, cMVc=cMVc,
                   cRSc=cRSc):
                scalar.wait_ge(cMVc.sem, vMV)
                scalar.activation(lnvv[:], mvv[:, 1:2], AF.Ln, bias=LN_EPS,
                                  scale=1.0)
                scalar.drain()
                scalar.activation(rsv[:], lnvv[:], AF.Exp, bias=0.0,
                                  scale=-0.5).then_inc(cRSc.sem, 1)
            SE(_f)
            vNB = cNBc.inc()
            def _f(vector, vRS=vRS, mvv=mvv, rsv=rsv, nbv=nbv, cRSc=cRSc,
                   cNBc=cNBc):
                vector.wait_ge(cRSc.sem, vRS)
                vector.tensor_scalar(nbv[:], mvv[:, 0:1], rsv[:], -1.0,
                                     op0=OP.mult, op1=OP.mult
                                     ).then_inc(cNBc.sem, 1)
            VE(_f)
            return cNBc, vNB, rsv, nbv

        cLD = C("cLD")
        cXp = [C("cXa"), C("cXb")]
        cXp[0].inc(16)
        cXp[1].inc(16)
        def _f(sync):
            sync.dma_start(ident[:], ident_t.ap()).then_inc(cLD.sem, 16)
            sync.dma_start(d2h[:], d2h_t.ap()).then_inc(cLD.sem, 16)
            sync.dma_start(rgate[:], rgate_t.ap()).then_inc(cLD.sem, 16)
            sync.dma_start(epair[:], epair_t.ap()).then_inc(cLD.sem, 16)
            sync.dma_start(cst[:], cst_t.ap()).then_inc(cLD.sem, 16)
            sync.dma_start(xb[0][:], x_h[0:128, :]).then_inc(cXp[0].sem, 16)
            sync.dma_start(xb[1][:], x_h[128:256, :]).then_inc(cXp[1].sem, 16)
            sync.dma_start(pm2[:], pm2_h).then_inc(cLD.sem, 16)
            sync.wait_ge(cLD.sem, 96)
        SP(_f)
        cLD.inc(96)
        barrier()

        h1 = poolA
        k_fm = poolB
        wqb = [poolC[:, 0:1024], poolC[:, 1024:2048]]
        xb = [poolC[:, 2048:3072], poolC[:, 3072:4096]]
        h1tb = [poolC[:, 4096:5120], poolC[:, 5120:6144]]
        qob = [poolC[:, 6144:6656], poolC[:, 6656:7168]]
        vtmp = poolC[:, 7168:7680]
        vtout = poolC[:, 7680:8192]

        cAPP = C("cAPP"); cH1T = C("cH1T"); cH1V = C("cH1V")

        def emit_a0_evac(st):
            cH1V.inc()
            def _f(vector, st=st):
                vector.wait_ge(cH1T.sem, st + 1)
                pbase = (st % 2) * 2
                last = None
                for c in range(NC):
                    b = pb[pbase + c // 4]
                    last = vector.tensor_copy(
                        h1[:, c * 2048 + st * 128:c * 2048 + st * 128 + 128],
                        b[:, (c % 4) * 128:(c % 4) * 128 + 128])
                last.then_inc(cH1V.sem, 1)
            VE(_f)

        for st in range(NST):
            cXc = cXp[st % 2]
            if st >= 2:
                vX = cXc.inc(16)
                def _f(sync, st=st, cXc=cXc):
                    sync.wait_ge(cAPP.sem, st - 1)
                    sync.dma_start(xb[st % 2][:],
                                   x_h[st * 128:(st + 1) * 128, :]
                                   ).then_inc(cXc.sem, 16)
                SP(_f)
            else:
                vX = 16
            def _f(vector, cXc=cXc, vX=vX):
                vector.wait_ge(cXc.sem, vX)
            VE(_f)
            cNBc, vNB, rsv, nbv = ln_stats_chain(xb[st % 2][:])
            vAPP = cAPP.inc()
            def _f(scalar, st=st, cNBc=cNBc, vNB=vNB, rsv=rsv, nbv=nbv):
                scalar.wait_ge(cNBc.sem, vNB)
                if st >= 2:
                    scalar.wait_ge(cH1T.sem, st - 1)
                scalar.activation(h1tb[st % 2][:], xb[st % 2][:], AF.Identity,
                                  bias=nbv[:], scale=rsv[:]).then_inc(cAPP.sem, 1)
            SE(_f)
            cH1T.inc()
            def _f(tensor, st=st, vAPP=vAPP):
                tensor.wait_ge(cAPP.sem, vAPP)
                if st >= 2:
                    tensor.wait_ge(cH1V.sem, st - 1)
                pbase = (st % 2) * 2
                last = None
                for c in range(NC):
                    b = pb[pbase + c // 4]
                    last = tensor.transpose(
                        b[:, (c % 4) * 128:(c % 4) * 128 + 128],
                        h1tb[st % 2][:, c * 128:c * 128 + 128], ident[:])
                last.then_inc(cH1T.sem, 1)
            TE(_f)
            if st >= 1:
                emit_a0_evac(st - 1)
        emit_a0_evac(NST - 1)
        barrier()

        cWQp = [C("cWQa"), C("cWQb")]
        cQKM = C("cQKM"); cQKE = C("cQKE")
        cQSp = [C("cQSa"), C("cQSb")]
        cVTa = C("cVTa"); cVT2 = C("cVT2"); cVTb = C("cVTb"); cVS = C("cVS")
        qg = [0]
        vg = [0]
        for oc in range(24):
            cWQc = cWQp[oc % 2]
            vWQ = cWQc.inc(16)
            def _f(sync, oc=oc, cWQc=cWQc):
                if oc >= 2:
                    sync.wait_ge(cQKM.sem, (oc - 1) * NT)
                src = bass.AP(wqkv_h.tensor, oc * 128,
                              [[3 * D, 128], [128 * 3 * D, 8], [1, 128]])
                sync.dma_start(
                    wqb[oc % 2][:].rearrange("p (c m) -> p c m", c=8),
                    src).then_inc(cWQc.sem, 16)
            SP(_f)
            for tile in range(NT):
                g = oc * NT + tile
                bank = pb[g % 2]
                vQKM = cQKM.inc()
                def _f(tensor, oc=oc, tile=tile, g=g, bank=bank, vWQ=vWQ,
                       cWQc=cWQc):
                    tensor.wait_ge(cWQc.sem, vWQ)
                    if g >= 2:
                        tensor.wait_ge(cQKE.sem, g - 1)
                    last = None
                    for ic in range(8):
                        last = tensor.matmul(
                            bank[:, :],
                            wqb[oc % 2][:, ic * 128:ic * 128 + 128],
                            h1[:, ic * 2048 + tile * 512:
                               ic * 2048 + tile * 512 + 512],
                            start=(ic == 0), stop=(ic == 7))
                    last.then_inc(cQKM.sem, 1)
                TE(_f)
                if oc < 8:
                    q = qg[0]; qg[0] += 1
                    cQSc = cQSp[q % 2]
                    vQKE = cQKE.inc()
                    def _f(vector, q=q, bank=bank, vQKM=vQKM, cQSc=cQSc):
                        vector.wait_ge(cQKM.sem, vQKM)
                        if q >= 2:
                            vector.wait_ge(cQSc.sem, 16 * (q // 2))
                        vector.tensor_copy(qob[q % 2][:], bank[:]
                                           ).then_inc(cQKE.sem, 1)
                    VE(_f)
                    cQSc.inc(16)
                    def _f(gpsimd, oc=oc, tile=tile, q=q, vQKE=vQKE,
                           cQSc=cQSc):
                        gpsimd.wait_ge(cQKE.sem, vQKE)
                        gpsimd.dma_start(
                            qs_h.ap()[oc, :, tile * 512:tile * 512 + 512],
                            qob[q % 2][:]).then_inc(cQSc.sem, 16)
                    GP(_f)
                elif oc < 16:
                    cQKE.inc()
                    def _f(vector, oc=oc, tile=tile, bank=bank, vQKM=vQKM):
                        vector.wait_ge(cQKM.sem, vQKM)
                        c = oc - 8
                        vector.tensor_copy(
                            k_fm[:, c * 2048 + tile * 512:
                                 c * 2048 + tile * 512 + 512],
                            bank[:]).then_inc(cQKE.sem, 1)
                    VE(_f)
                else:
                    v = vg[0]; vg[0] += 1
                    vVTa = cVTa.inc()
                    cQKE.inc()
                    def _f(vector, v=v, bank=bank, vQKM=vQKM):
                        vector.wait_ge(cQKM.sem, vQKM)
                        if v >= 1:
                            vector.wait_ge(cVT2.sem, v)
                        vector.tensor_copy(vtmp[:], bank[:]).then_inc(cVTa.sem, 1)
                        vector.drain().then_inc(cQKE.sem, 1)
                    VE(_f)
                    vVT2 = cVT2.inc()
                    def _f(tensor, v=v, vVTa=vVTa):
                        tensor.wait_ge(cVTa.sem, vVTa)
                        if v >= 1:
                            tensor.wait_ge(cVTb.sem, v)
                        last = None
                        for j in range(4):
                            last = tensor.transpose(
                                pb[7][:, j * 128:j * 128 + 128],
                                vtmp[:, j * 128:j * 128 + 128], ident[:])
                        last.then_inc(cVT2.sem, 1)
                    TE(_f)
                    vVTb = cVTb.inc()
                    def _f(vector, v=v, vVT2=vVT2):
                        vector.wait_ge(cVT2.sem, vVT2)
                        if v >= 1:
                            vector.wait_ge(cVS.sem, 64 * v)
                        vector.tensor_copy(vtout[:], pb[7][:]
                                           ).then_inc(cVTb.sem, 1)
                    VE(_f)
                    cVS.inc(64)
                    def _f(gpsimd, oc=oc, tile=tile, vVTb=vVTb):
                        gpsimd.wait_ge(cVTb.sem, vVTb)
                        for j in range(4):
                            gpsimd.dma_start(
                                vs_h[tile * 512 + j * 128:
                                     tile * 512 + j * 128 + 128,
                                     (oc - 16) * 128:(oc - 16) * 128 + 128],
                                vtout[:, j * 128:j * 128 + 128]
                            ).then_inc(cVS.sem, 16)
                    GP(_f)
        barrier()

        kf_sb = poolC
        vld = [poolC[:, 4096:5120], poolC[:, 5120:6144]]
        kvT = poolC[:, 6144:6656]
        k2b = [poolC[:, 6656:7168], poolC[:, 7168:7680]]
        dk_sb = poolC[:, 7680:8192]

        cK2 = C("cK2"); cDKM = C("cDKM"); cDKE = C("cDKE"); cDKT = C("cDKT")
        cGKE = C("cGKE")
        vQSa_0, vQSb_0, vVS_0 = cQSp[0].n, cQSp[1].n, cVS.n
        v_aug = poolA
        def _vdst(st):
            return bass.AP(poolA, st * 1040, [[16640, 128], [65, 16], [1, 64]])
        def _vsrc(st):
            return vs_h[st * 128:(st + 1) * 128, :].rearrange(
                "p (h j) -> p h j", h=16)
        cVLp = [C("cVLa"), C("cVLb")]
        dma_wait = {}
        def _next(chan, st, first=False):
            pre = None if first else chan.n
            v = chan.inc(16)
            dma_wait[st] = (chan, v)
            return (st, pre, chan)
        def _vhost_off(st):
            return st * 1024 if st < 4 else 4096 + (st - 4) * 1024
        def _emit_issue(eng, item):
            st, pre, chan = item
            if pre is not None:
                eng.wait_ge(chan.sem, pre)
            if st < 6:
                dst = poolC[:, _vhost_off(st):_vhost_off(st) + 1024]
                eng.dma_start(dst, vs_h[st * 128:(st + 1) * 128, :]
                              ).then_inc(chan.sem, 16)
            else:
                eng.dma_start(_vdst(st), _vsrc(st)).then_inc(chan.sem, 16)
        se_plan = [_next(cVT2, st, True) for st in (1, 3, 5)]
        sp_plan = ([_next(cVTa, st, True) for st in (0, 2, 4)]
                   + [_next(cVLp[0], st, True) for st in (6, 8, 10, 12, 14)])
        gp_plan = ([_next(cQSp[1], st, True) for st in (7, 11, 15)]
                   + [_next(cQSp[0], st, True) for st in (9, 13)])
        vPreA = cVTa.n
        vPreB = cVT2.n
        vStrA = cVLp[0].n
        vStrB = cQSp[1].n
        vStrC = cQSp[0].n
        def _f(scalar, a=vQSa_0, b=vQSb_0, v=vVS_0, plan=se_plan):
            scalar.wait_ge(cQSp[0].sem, a)
            scalar.wait_ge(cQSp[1].sem, b)
            scalar.wait_ge(cVS.sem, v)
            for item in plan:
                _emit_issue(scalar, item)
        SE(_f)
        def _f(sync, a=vQSa_0, b=vQSb_0, v=vVS_0, plan=sp_plan):
            sync.wait_ge(cQSp[0].sem, a)
            sync.wait_ge(cQSp[1].sem, b)
            sync.wait_ge(cVS.sem, v)
            for item in plan:
                _emit_issue(sync, item)
        SP(_f)
        def _f(gpsimd, a=vQSa_0, b=vQSb_0, v=vVS_0, plan=gp_plan):
            gpsimd.wait_ge(cVS.sem, v)
            gpsimd.wait_ge(cQSp[0].sem, a)
            gpsimd.wait_ge(cQSp[1].sem, b)
            for item in plan:
                _emit_issue(gpsimd, item)
        GP(_f)
        def _f(vector):
            vector.memset(bpad[:], 0.0)
        VE(_f)
        for tile in range(NT):
            for c in range(NC):
                vK2 = cK2.inc()
                def _f(scalar, tile=tile, c=c, vK2=vK2):
                    if vK2 >= 3:
                        scalar.wait_ge(cDKM.sem, vK2 - 2)
                    scalar.activation(k2b[c % 2][:],
                                      k_fm[:, c * 2048 + tile * 512:
                                           c * 2048 + tile * 512 + 512],
                                      AF.Square, bias=0.0, scale=1.0
                                      ).then_inc(cK2.sem, 1)
                SE(_f)
                vDKM = cDKM.inc()
                def _f(tensor, tile=tile, c=c, vK2=vK2):
                    tensor.wait_ge(cK2.sem, vK2)
                    if tile > 0 and c == 0:
                        tensor.wait_ge(cDKE.sem, tile)
                    tensor.matmul(pb[4][0:16, :], d2h[:, c * 16:c * 16 + 16],
                                  k2b[c % 2][:], start=(c == 0), stop=(c == 7)
                                  ).then_inc(cDKM.sem, 1)
                TE(_f)
            vDKE = cDKE.inc()
            def _f(scalar, tile=tile, vDKE=vDKE):
                scalar.wait_ge(cDKM.sem, tile * 8 + 8)
                if tile > 0:
                    scalar.wait_ge(cDKT.sem, tile)
                scalar.activation(dk_sb[0:16, :], pb[4][0:16, :], AF.Identity,
                                  bias=0.0, scale=1.0).then_inc(cDKE.sem, 1)
            SE(_f)
            vDKT = cDKT.inc()
            def _f(tensor, tile=tile, vDKE=vDKE):
                tensor.wait_ge(cDKE.sem, vDKE)
                if tile > 0:
                    tensor.wait_ge(cGKE.sem, tile)
                last = None
                for j in range(4):
                    last = tensor.transpose(pb[2][:, j * 16:j * 16 + 16],
                                            dk_sb[0:16, j * 128:j * 128 + 128],
                                            ident[0:16, 0:16])
                last.then_inc(cDKT.sem, 1)
            TE(_f)
            cGKE.inc()
            def _f(scalar, tile=tile, vDKT=vDKT):
                scalar.wait_ge(cDKT.sem, vDKT)
                scalar.activation(gk[:, tile * 64:tile * 64 + 64],
                                  pb[2][:, 0:64], AF.Exp, bias=0.0,
                                  scale=-0.5).then_inc(cGKE.sem, 1)
            SE(_f)

        cVA = C("cVA")
        for st in range(NST):
            cVA.inc()
            def _f(vector, st=st):
                if st < 6:
                    vector.wait_ge(cVTa.sem, vPreA)
                    vector.wait_ge(cVT2.sem, vPreB)
                else:
                    vector.wait_ge(cVLp[0].sem, vStrA)
                    vector.wait_ge(cQSp[1].sem, vStrB)
                    vector.wait_ge(cQSp[0].sem, vStrC)
                vector.wait_ge(cGKE.sem, st // 4 + 1)
                out_ap = bass.AP(poolA, st * 1040,
                                 [[16640, 128], [65, 16], [1, 64]])
                in_ap = (bass.AP(poolC, _vhost_off(st),
                                 [[8192, 128], [64, 16], [1, 64]])
                         if st < 6 else out_ap)
                gk_b = bass.AP(gk, st * 16, [[256, 128], [1, 16], [0, 64]])
                vector.tensor_tensor(out_ap, in_ap, gk_b, op=OP.mult)
                vector.drain()
                gcols = bass.AP(poolA, st * 1040 + 64, [[16640, 128], [65, 16]])
                vector.tensor_copy(gcols, gk[:, st * 16:st * 16 + 16]
                                   ).then_inc(cVA.sem, 1)
            VE(_f)
        barrier()

        cUK = C("cUK"); cKFe = C("cKFe"); cKFr = C("cKFr")
        cSTB = C("cSTB"); cKVP = C("cKVP")
        cKVE = C("cKVE"); cKVT = C("cKVT"); cKVAE = C("cKVAE")

        def emit_kv_part(h, stg):
            kvbank = pb[5 + h % 2]
            vKVP = cKVP.inc()
            def _f(tensor, h=h, stg=stg, kvbank=kvbank):
                tensor.wait_ge(cKFe.sem, 8 * h + 2 * (stg + 1))
                if stg == 0 and h >= 2:
                    tensor.wait_ge(cKVE.sem, h - 1)
                last = None
                for j in range(4):
                    stt = stg * 4 + j
                    last = tensor.matmul(kvbank[0:65, 0:256],
                                         v_aug[:, stt * 1040 + h * 65:
                                               stt * 1040 + h * 65 + 65],
                                         kf_sb[:, stt * 256:stt * 256 + 256],
                                         start=(stt == 0),
                                         stop=(stt == NST - 1))
                last.then_inc(cKVP.sem, 1)
            TE(_f)

        def emit_kv_evac(h):
            kvbank = pb[5 + h % 2]
            vKVE = cKVE.inc()
            def _f(scalar, h=h, kvbank=kvbank, vKVE=vKVE):
                scalar.wait_ge(cKVP.sem, 4 * (h + 1))
                if vKVE >= 2:
                    scalar.wait_ge(cKVT.sem, vKVE - 1)
                scalar.activation(kvT[0:65, 0:256], kvbank[0:65, 0:256],
                                  AF.Identity, bias=0.0, scale=1.0
                                  ).then_inc(cKVE.sem, 1)
            SE(_f)
            vKVT = cKVT.inc()
            def _f(tensor, vKVE=vKVE, vKVT=vKVT):
                tensor.wait_ge(cKVE.sem, vKVE)
                if vKVT >= 2:
                    tensor.wait_ge(cKVAE.sem, vKVT - 1)
                last = None
                for half in range(2):
                    last = tensor.transpose(pb[7][:, half * 65:half * 65 + 65],
                                            kvT[0:65, half * 128:half * 128 + 128],
                                            ident[0:65, 0:65])
                last.then_inc(cKVT.sem, 1)
            TE(_f)
            cKVAE.inc()
            def _f(vector, h=h, vKVT=vKVT):
                vector.wait_ge(cKVT.sem, vKVT)
                vector.tensor_copy(kvA[:, h * 130:h * 130 + 130],
                                   pb[7][:, 0:130])
                last = None
                for half in range(2):
                    last = vector.tensor_copy(
                        bpad[:, (2 * h + half) * 16 + h:
                             (2 * h + half) * 16 + h + 1],
                        pb[7][:, half * 65 + 64:half * 65 + 65])
                last.then_inc(cKVAE.sem, 1)
            VE(_f)

        for h in range(H):
            c = h // 2
            ro = (h % 2) * 64
            for stg in range(4):
                bank = pb[2 + stg % 2]
                vUK = cUK.inc()
                def _f(tensor, h=h, c=c, ro=ro, stg=stg, bank=bank, vUK=vUK):
                    if vUK >= 3:
                        tensor.wait_ge(cKFe.sem, 2 * (vUK - 2))
                        tensor.wait_ge(cSTB.sem, vUK - 2)
                    last = None
                    for j in range(4):
                        stt = stg * 4 + j
                        last = tensor.matmul(
                            bank[:, j * 128:j * 128 + 128],
                            k_fm[ro:ro + 64,
                                 c * 2048 + stt * 128:c * 2048 + stt * 128 + 128],
                            pm2[ro:ro + 64, h * 128:h * 128 + 128],
                            start=True, stop=True)
                    last.then_inc(cUK.sem, 1)
                TE(_f)
                vKF = cKFe.inc(2)
                def _f(scalar, h=h, stg=stg, bank=bank, vUK=vUK):
                    scalar.wait_ge(cUK.sem, vUK)
                    if h == 0:
                        scalar.wait_ge(cVA.sem, stg + 1)
                    if h >= 1:
                        scalar.wait_ge(cKVP.sem, 4 * (h - 1) + stg + 1)
                    for sgn in range(2):
                        out_ap = bass.AP(poolC, stg * 4 * 256 + sgn * 128,
                                         [[8192, 128], [256, 4], [1, 128]])
                        in_ap = bank[:].rearrange("p (j f) -> p j f", j=4)
                        scalar.activation(out_ap, in_ap, AF.Exp, bias=0.0,
                                          scale=(0.125 if sgn == 0 else -0.125)
                                          ).then_inc(cKFe.sem, 1)
                SE(_f)
                cSTB.inc()
                def _f(vector, h=h, stg=stg, bank=bank, vUK=vUK, vKF=vKF):
                    vector.wait_ge(cUK.sem, vUK)
                    vector.wait_ge(cKFe.sem, vKF)
                    vector.drain()
                    if stg == 0:
                        vector.reduce_max(stabk_raw[:, h:h + 1], bank[:],
                                          axis=AX).then_inc(cSTB.sem, 1)
                    else:
                        vector.reduce_max(smax[:], bank[:], axis=AX)
                        vector.drain()
                        vector.tensor_tensor(stabk_raw[:, h:h + 1],
                                             stabk_raw[:, h:h + 1], smax[:],
                                             op=OP.max).then_inc(cSTB.sem, 1)
                VE(_f)
                if stg >= 1:
                    emit_kv_part(h, stg - 1)
                elif h >= 1:
                    emit_kv_part(h - 1, 3)
                    emit_kv_evac(h - 1)
        emit_kv_part(H - 1, 3)
        emit_kv_evac(H - 1)

        cSK = C("cSK"); cBP = C("cBP")
        cSK.inc()
        def _f(tensor):
            tensor.wait_ge(cSTB.sem, 64)
            tensor.transpose(pb[2][0:16, 0:128], stabk_raw[:], ident[:]
                             ).then_inc(cSK.sem, 1)
        TE(_f)
        cBP.inc()
        def _f(vector):
            vector.wait_ge(cSK.sem, 1)
            vector.reduce_max(stk_t[:, 0:1], pb[2][0:16, 0:128], axis=AX)
            vector.drain()
            vector.tensor_scalar(stk_t[:, 1:2], stk_t[:, 0:1], 0.125, LN256EPS,
                                 op0=OP.mult, op1=OP.add).then_inc(cBP.sem, 1)
        VE(_f)

        wout_sb = poolA[:, 0:8192]
        q_sb = poolA[:, 8192:12288]
        cWO = C("cWO"); cWO2 = C("cWO2")
        cQLp = [C("cQLa"), C("cQLb")]
        cWO.inc(16)
        cQLp[0].inc(16)
        cQLp[1].inc(16)
        vQSa_t, vQSb_t = cQSp[0].n, cQSp[1].n
        cWO2.inc(16)
        def _f(sync, vQSa_t=vQSa_t, vQSb_t=vQSb_t):
            sync.wait_ge(cKVP.sem, 64)
            sync.wait_ge(cQSp[0].sem, vQSa_t)
            sync.wait_ge(cQSp[1].sem, vQSb_t)
            src = bass.AP(qs_h, 0, [[N, 128], [128 * N, 4], [1, 512]])
            sync.dma_start(
                q_sb[:, 0:2048].rearrange("p (c m) -> p c m", c=4),
                src).then_inc(cQLp[0].sem, 16)
            src = bass.AP(wout_h.tensor, 0, [[D, 128], [128 * D, 4], [1, D]])
            sync.dma_start(
                wout_sb[:, 0:4096].rearrange("p (c m) -> p c m", c=4),
                src).then_inc(cWO.sem, 16)
        SP(_f)
        def _f(scalar, vQSa_t=vQSa_t, vQSb_t=vQSb_t):
            scalar.wait_ge(cKVP.sem, 64)
            scalar.wait_ge(cQSp[0].sem, vQSa_t)
            scalar.wait_ge(cQSp[1].sem, vQSb_t)
            src = bass.AP(qs_h, 4 * 128 * N, [[N, 128], [128 * N, 4], [1, 512]])
            scalar.dma_start(
                q_sb[:, 2048:4096].rearrange("p (c m) -> p c m", c=4),
                src).then_inc(cQLp[1].sem, 16)
            src = bass.AP(wout_h.tensor, 4 * 128 * D,
                          [[D, 128], [128 * D, 4], [1, D]])
            scalar.dma_start(
                wout_sb[:, 4096:8192].rearrange("p (c m) -> p c m", c=4),
                src).then_inc(cWO2.sem, 16)
        SE(_f)
        barrier()

        qf_hosts = [poolB[:, (8 + 2 * t) * 1024:(8 + 2 * t) * 1024 + 2048]
                    for t in range(NT)]
        rb_sb = poolA[:, 14336:14848]
        sfm = poolC[0:16, 7680:8192]
        attnT = poolA[:, 12288:16384]
        xb2 = [poolC[:, 0:1024], poolC[:, 1024:2048]]
        attn_raw = poolC[:, 2048:6144]
        ln2dst = poolC[:, 2048:3072]
        wo_tmp = poolC[:, 6144:6656]
        q2b = [poolC[:, 6656:7168], poolC[:, 7168:7680]]
        eterm = poolC[0:16, 7168:7680]
        stq = poolA[:, 16384:16448]
        x1 = poolB

        cX2p = [C("cX2a"), C("cX2b")]
        cQ2 = C("cQ2"); cDQM = C("cDQM"); cQFe = C("cQFe"); cQFr = C("cQFr")
        cDQE = C("cDQE"); cUQT = C("cUQT"); cRMX = C("cRMX"); cSTT = C("cSTT")
        cSFM = C("cSFM"); cET = C("cET"); cUQ = C("cUQ")
        cQF = C("cQF"); cAM = C("cAM"); cAE = C("cAE"); cBM = C("cBM")
        cDR = C("cDR"); cRBM = C("cRBM"); cRBE = C("cRBE"); cSCL = C("cSCL")
        cWOM = C("cWOM"); cWOE = C("cWOE"); cWOT = C("cWOT"); cWOTE = C("cWOTE")
        cX1 = C("cX1"); cAP2 = C("cAP2")

        def emit_C1(tile):
            vQL = 16 * (tile + 1)
            for c in range(NC):
                vQ2 = cQ2.inc()
                def _f(scalar, tile=tile, c=c, vQL=vQL, vQ2=vQ2):
                    scalar.wait_ge(cQLp[c // 4].sem, vQL)
                    if c < 2 and tile > 0:
                        scalar.wait_ge(cDQM.sem, 8 * tile)
                    if c == 1 and tile > 0:
                        scalar.wait_ge(cRBM.sem, 8 * tile)
                    if vQ2 >= 3:
                        scalar.wait_ge(cDQM.sem, vQ2 - 2)
                    scalar.activation(q2b[c % 2][:],
                                      q_sb[:, c * 512:c * 512 + 512],
                                      AF.Square, bias=0.0, scale=1.0
                                      ).then_inc(cQ2.sem, 1)
                SE(_f)
                vDQM = cDQM.inc()
                def _f(tensor, tile=tile, c=c, vQ2=vQ2):
                    tensor.wait_ge(cQ2.sem, vQ2)
                    if c == 0 and tile > 0:
                        tensor.wait_ge(cDR.sem, tile)
                    tensor.matmul(pb[4][0:16, :], d2h[:, c * 16:c * 16 + 16],
                                  q2b[c % 2][:], start=(c == 0), stop=(c == 7)
                                  ).then_inc(cDQM.sem, 1)
                TE(_f)
            vDQE = cDQE.inc()
            def _f(scalar, tile=tile):
                scalar.wait_ge(cDQM.sem, tile * 8 + 8)
                if tile > 0:
                    scalar.wait_ge(cSFM.sem, tile)
                scalar.activation(sfm[:, :], pb[4][0:16, :], AF.Identity,
                                  bias=0.0, scale=0.5).then_inc(cDQE.sem, 1)
            SE(_f)
            for h in range(H):
                c = h // 2
                ro = (h % 2) * 64
                bank = pb[2 + h % 2]
                vUQT = cUQT.inc()
                def _f(tensor, h=h, c=c, ro=ro, bank=bank, vUQT=vUQT,
                       vQL=vQL, tile=tile):
                    tensor.wait_ge(cQLp[c // 4].sem, vQL)
                    if h == 0 and tile > 0:
                        tensor.wait_ge(cWOTE.sem, 8 * tile)
                    if vUQT >= 3:
                        tensor.wait_ge(cRMX.sem, vUQT - 2)
                    last = None
                    for j in range(4):
                        last = tensor.matmul(
                            bank[:, j * 128:j * 128 + 128],
                            q_sb[ro:ro + 64,
                                 c * 512 + j * 128:c * 512 + j * 128 + 128],
                            pm2[ro:ro + 64, h * 128:h * 128 + 128],
                            start=True, stop=True)
                    last.then_inc(cUQT.sem, 1)
                TE(_f)
                cRMX.inc()
                def _f(vector, h=h, bank=bank, vUQT=vUQT, tile=tile):
                    vector.wait_ge(cUQT.sem, vUQT)
                    if tile > 0 and h == 0:
                        vector.wait_ge(cSTT.sem, tile)
                    in_ap = bank[:].rearrange("p (j f) -> p j f", j=4)
                    out_ap = bass.AP(poolA, 16384 + h, [[16640, 128], [16, 4]])
                    vector.reduce_max(out_ap, in_ap, axis=AX
                                      ).then_inc(cRMX.sem, 1)
                VE(_f)
            vSTT = cSTT.inc()
            def _f(tensor, tile=tile, vDQE=vDQE):
                tensor.wait_ge(cRMX.sem, tile * 16 + 16)
                tensor.wait_ge(cDQE.sem, vDQE)
                last = None
                for j in range(4):
                    last = tensor.transpose(
                        pb[4][0:16, j * 128:j * 128 + 128],
                        stq[:, j * 16:j * 16 + 16], ident[:])
                last.then_inc(cSTT.sem, 1)
            TE(_f)
            vSFM = cSFM.inc()
            def _f(vector, vSTT=vSTT):
                vector.wait_ge(cSTT.sem, vSTT)
                vector.scalar_tensor_tensor(sfm[:, :], pb[4][0:16, :], 0.125,
                                            sfm[:, :], op0=OP.mult,
                                            op1=OP.add).then_inc(cSFM.sem, 1)
            VE(_f)
            vET = cET.inc()
            def _f(scalar, vSFM=vSFM, tile=tile):
                scalar.wait_ge(cSFM.sem, vSFM)
                if tile == 0:
                    scalar.wait_ge(cBP.sem, 1)
                else:
                    scalar.wait_ge(cRBM.sem, 8 * tile)
                scalar.activation(eterm[:, :], sfm[:, :], AF.Exp,
                                  bias=stk_t[:, 1:2], scale=1.0
                                  ).then_inc(cET.sem, 1)
            SE(_f)

        uq_v = {}
        ab_v = {}

        def emit_C2(tile, part="all"):
            def emit_AB_mm(pr):
                gp = tile * 8 + pr
                abank = pb[pr % 2]
                vAM = cAM.inc()
                ab_v[(tile, pr)] = vAM
                def _f(tensor, pr=pr, gp=gp, abank=abank, vAM=vAM, tile=tile):
                    tensor.wait_ge(cQFe.sem, gp * 4 + 4)
                    if pr < 2 and tile > 0:
                        tensor.wait_ge(cWOE.sem, 8 * tile)
                    if vAM >= 3:
                        tensor.wait_ge(cAE.sem, vAM - 2)
                    qf_sb = qf_hosts[tile]
                    last = None
                    for hh in range(2):
                        h = 2 * pr + hh
                        for half in range(2):
                            last = tensor.matmul(
                                abank[hh * 64:hh * 64 + 64, :],
                                kvA[:, h * 130 + half * 65:
                                    h * 130 + half * 65 + 64],
                                qf_sb[:, (2 * hh + half) * 512:
                                      (2 * hh + half) * 512 + 512],
                                start=(half == 0), stop=(half == 1))
                    last.then_inc(cAM.sem, 1)
                TE(_f)
                cBM.inc()
                def _f(tensor, pr=pr, tile=tile):
                    if pr == 0:
                        tensor.wait_ge(cSFM.sem, tile + 1)
                        if tile > 0:
                            tensor.wait_ge(cDR.sem, tile)
                    qf_sb = qf_hosts[tile]
                    last = None
                    for hh in range(2):
                        h = 2 * pr + hh
                        for half in range(2):
                            last = tensor.matmul(
                                pb[4][0:16, :],
                                bpad[:, (2 * h + half) * 16:
                                     (2 * h + half) * 16 + 16],
                                qf_sb[:, (2 * hh + half) * 512:
                                      (2 * hh + half) * 512 + 512],
                                start=(pr == 0 and hh == 0 and half == 0),
                                stop=(pr == 7 and hh == 1 and half == 1))
                    last.then_inc(cBM.sem, 1)
                TE(_f)
                return vAM

            def emit_AB_evac(pr, vAM=None):
                abank = pb[pr % 2]
                if vAM is None:
                    vAM = ab_v[(tile, pr)]
                vAE = cAE.inc()
                def _f(vector, pr=pr, abank=abank, vAM=vAM, tile=tile):
                    vector.wait_ge(cAM.sem, vAM)
                    if tile > 0:
                        vector.wait_ge(cWOM.sem, 8 * tile)
                    if pr == 0:
                        vector.drain()
                    vector.tensor_copy(attn_raw[:, pr * 512:pr * 512 + 512],
                                       abank[:]).then_inc(cAE.sem, 1)
                VE(_f)

            def emit_AB(pr):
                emit_AB_evac(pr, emit_AB_mm(pr))

            def emit_uq(pr, hh):
                h = 2 * pr + hh
                c = h // 2
                ro = (h % 2) * 64
                bank = pb[2 + hh]
                vUQ = cUQ.inc()
                uq_v[(tile, pr, hh)] = vUQ
                def _f(tensor, h=h, c=c, ro=ro, bank=bank, vUQ=vUQ,
                       pr=pr, hh=hh, tile=tile):
                    if pr == 0 and hh == 0:
                        tensor.wait_ge(cRMX.sem, 16 * (tile + 1))
                    if vUQ >= 3:
                        tensor.wait_ge(cQFe.sem, 2 * (vUQ - 2))
                    tensor.matmul(bank[:],
                                  pm2[ro:ro + 64, h * 128:h * 128 + 128],
                                  q_sb[ro:ro + 64, c * 512:c * 512 + 512],
                                  start=True, stop=True
                                  ).then_inc(cUQ.sem, 1)
                TE(_f)

            def emit_ex(pr, hh):
                gp = tile * 8 + pr
                bank = pb[2 + hh]
                vUQ = uq_v[(tile, pr, hh)]
                cQFe.inc(2)
                def _f(scalar, hh=hh, bank=bank, vUQ=vUQ, gp=gp,
                       pr=pr, tile=tile):
                    scalar.wait_ge(cUQ.sem, vUQ)
                    if gp >= 1:
                        scalar.wait_ge(cAM.sem, gp)
                        scalar.wait_ge(cBM.sem, gp)
                    qf_sb = qf_hosts[tile]
                    for sgn in range(2):
                        scalar.activation(
                            qf_sb[:, (2 * hh + sgn) * 512:
                                  (2 * hh + sgn) * 512 + 512],
                            bank[:], AF.Exp, bias=0.0,
                            scale=(0.125 if sgn == 0 else -0.125)
                            ).then_inc(cQFe.sem, 1)
                SE(_f)

            if part == "head":
                for pr in (0, 1):
                    emit_uq(pr, 0); emit_ex(pr, 0)
                    emit_uq(pr, 1); emit_ex(pr, 1)
                emit_AB_mm(0)
                emit_uq(2, 0)
                emit_uq(2, 1)
                return
            if part == "rest":
                emit_ex(2, 0)
                emit_ex(2, 1)
                emit_AB_evac(0)
                emit_AB(1)
                pr_start = 3
            else:
                pr_start = 0
            for pr in range(pr_start, 8):
                for hh in range(2):
                    emit_uq(pr, hh)
                    emit_ex(pr, hh)
                if pr >= 1:
                    emit_AB(pr - 1)
            emit_AB(7)
            if tile + 1 < NT:
                cQLp[0].inc(16)
                cQLp[1].inc(16)
                def _f(sync, tile=tile):
                    sync.wait_ge(cQ2.sem, 8 * (tile + 1))
                    sync.wait_ge(cUQT.sem, 16 * (tile + 1))
                    sync.wait_ge(cUQ.sem, 16 * (tile + 1))
                    src = bass.AP(qs_h, (tile + 1) * 512,
                                  [[N, 128], [128 * N, 4], [1, 512]])
                    sync.dma_start(
                        q_sb[:, 0:2048].rearrange("p (c m) -> p c m", c=4),
                        src).then_inc(cQLp[0].sem, 16)
                    src = bass.AP(qs_h, 4 * 128 * N + (tile + 1) * 512,
                                  [[N, 128], [128 * N, 4], [1, 512]])
                    sync.dma_start(
                        q_sb[:, 2048:4096].rearrange("p (c m) -> p c m", c=4),
                        src).then_inc(cQLp[1].sem, 16)
                SP(_f)
            emit_x1_loads(tile)

        def emit_C3(tile):
            vDR = cDR.inc()
            def _f(vector, vDR=vDR, tile=tile):
                vector.wait_ge(cBM.sem, tile * 8 + 8)
                vector.wait_ge(cET.sem, tile + 1)
                vector.tensor_tensor(eterm[:, :], pb[4][0:16, :], eterm[:, :],
                                     op=OP.add)
                vector.drain()
                vector.reciprocal(eterm[:, :], eterm[:, :]
                                  ).then_inc(cDR.sem, 1)
            VE(_f)

            def emit_rbtail(pr):
                vRBE = cRBE.inc()
                rbank = pb[2 + pr % 2]
                def _f(scalar, rbank=rbank, pr=pr, tile=tile, vRBE=vRBE):
                    scalar.wait_ge(cRBM.sem, tile * 8 + pr + 1)
                    if vRBE >= 2:
                        scalar.wait_ge(cSCL.sem, vRBE - 1)
                    scalar.activation(rb_sb[:], rbank[:], AF.Identity,
                                      bias=0.0, scale=1.0).then_inc(cRBE.sem, 1)
                SE(_f)
                cSCL.inc()
                def _f(vector, pr=pr, vRBE=vRBE):
                    vector.wait_ge(cRBE.sem, vRBE)
                    vector.tensor_tensor(attn_raw[:, pr * 512:pr * 512 + 512],
                                         attn_raw[:, pr * 512:pr * 512 + 512],
                                         rb_sb[:], op=OP.mult
                                         ).then_inc(cSCL.sem, 1)
                VE(_f)

            for pr in range(8):
                rbank = pb[2 + pr % 2]
                vRBM = cRBM.inc()
                def _f(tensor, pr=pr, rbank=rbank, vDR=vDR, vRBM=vRBM,
                       tile=tile):
                    tensor.wait_ge(cDR.sem, vDR)
                    if pr == 0:
                        tensor.wait_ge(cQFe.sem, 32 * (tile + 1))
                    if vRBM >= 3:
                        tensor.wait_ge(cRBE.sem, vRBM - 2)
                    tensor.matmul(rbank[:], epair[:, pr * 128:pr * 128 + 128],
                                  eterm[:, :], start=True, stop=True
                                  ).then_inc(cRBM.sem, 1)
                TE(_f)
                if pr >= 1:
                    emit_rbtail(pr - 1)
            emit_rbtail(7)

        def emit_x1_loads(tile):
            for j in range(4):
                stt = tile * 4 + j
                cX2c = cX2p[j % 2]
                pre = cX2c.n if (tile > 0 or j >= 2) else None
                cX2c.inc(16)
                def _f(sync, stt=stt, cX2c=cX2c, pre=pre):
                    if pre is not None:
                        sync.wait_ge(cX2c.sem, pre)
                    if stt >= 8:
                        qt = (stt - 8) // 2
                        sync.wait_ge(cAM.sem, 8 * qt + 8)
                        sync.wait_ge(cBM.sem, 8 * qt + 8)
                    sync.dma_start(x1[:, stt * 1024:stt * 1024 + 1024],
                                   x_h[stt * 128:stt * 128 + 128, :]
                                   ).then_inc(cX2c.sem, 16)
                SP(_f)

        def emit_C4a(tile):
            for oc in range(NC):
                obank = pb[oc % 2]
                vWOM = cWOM.inc()
                def _f(tensor, oc=oc, obank=obank, vWOM=vWOM, tile=tile):
                    if oc == 0:
                        tensor.wait_ge(cWO.sem, 16)
                        tensor.wait_ge(cWO2.sem, 16)
                        tensor.wait_ge(cSCL.sem, tile * 8 + 8)
                    if vWOM >= 3:
                        tensor.wait_ge(cWOE.sem, vWOM - 2)
                    last = None
                    for ic in range(8):
                        last = tensor.matmul(
                            obank[:],
                            wout_sb[:, ic * 1024 + oc * 128:
                                    ic * 1024 + oc * 128 + 128],
                            attn_raw[:, ic * 512:ic * 512 + 512],
                            start=(ic == 0), stop=(ic == 7))
                    last.then_inc(cWOM.sem, 1)
                TE(_f)
                vWOE = cWOE.inc()
                def _f(vector, obank=obank, vWOM=vWOM, vWOE=vWOE):
                    vector.wait_ge(cWOM.sem, vWOM)
                    if vWOE >= 2:
                        vector.wait_ge(cWOT.sem, vWOE - 1)
                    vector.tensor_copy(wo_tmp[:], obank[:]).then_inc(cWOE.sem, 1)
                VE(_f)
                vWOT = cWOT.inc()
                def _f(tensor, oc=oc, vWOE=vWOE, vWOT=vWOT, tile=tile):
                    tensor.wait_ge(cWOE.sem, vWOE)
                    if oc == 0:
                        tensor.wait_ge(cRBE.sem, 8 * (tile + 1))
                    if vWOT >= 3:
                        tensor.wait_ge(cWOTE.sem, vWOT - 2)
                    b = pb[2 + oc % 2]
                    last = None
                    for j in range(4):
                        last = tensor.transpose(b[:, j * 128:j * 128 + 128],
                                                wo_tmp[:, j * 128:j * 128 + 128],
                                                ident[:])
                    last.then_inc(cWOT.sem, 1)
                TE(_f)
                cWOTE.inc()
                def _f(vector, oc=oc, vWOT=vWOT, tile=tile):
                    vector.wait_ge(cWOT.sem, vWOT)
                    if oc == 0 and tile > 0:
                        vector.wait_ge(cX1.sem, 4 * tile)
                    b = pb[2 + oc % 2]
                    last = None
                    for j in range(4):
                        last = vector.tensor_copy(
                            attnT[:, j * 1024 + oc * 128:
                                  j * 1024 + oc * 128 + 128],
                            b[:, j * 128:j * 128 + 128])
                    last.then_inc(cWOTE.sem, 1)
                VE(_f)

        def emit_C4b(tile):
            def emit_add(stt):
                j = stt % 4
                cX2c = cX2p[j % 2]
                cX1.inc()
                def _f(vector, stt=stt, j=j, cX2c=cX2c):
                    vector.wait_ge(cAP2.sem, stt + 1)
                    vector.wait_ge(cX2c.sem, 16 * (stt // 2 + 1))
                    xs = x1[:, stt * 1024:stt * 1024 + 1024]
                    vector.tensor_tensor(xs[:, 0:512], xs[:, 0:512],
                                         ln2dst[:, 0:512], op=OP.add)
                    vector.tensor_tensor(xs[:, 512:1024], xs[:, 512:1024],
                                         ln2dst[:, 512:1024],
                                         op=OP.add).then_inc(cX1.sem, 1)
                VE(_f)

            for j in range(4):
                stt = tile * 4 + j
                cX2c = cX2p[j % 2]
                def _f(vector, tile=tile):
                    vector.wait_ge(cWOTE.sem, tile * 8 + 8)
                VE(_f)
                cNBc, vNB, rsv, nbv = ln_stats_chain(
                    attnT[:, j * 1024:j * 1024 + 1024])
                cAP2.inc()
                def _f(scalar, j=j, cNBc=cNBc, vNB=vNB, rsv=rsv, nbv=nbv,
                       tile=tile, stt=stt):
                    scalar.wait_ge(cNBc.sem, vNB)
                    if j == 0:
                        scalar.wait_ge(cWOM.sem, 8 * (tile + 1))
                    if stt >= 1:
                        scalar.wait_ge(cX1.sem, stt)
                    scalar.activation(ln2dst[:, 0:512],
                                      attnT[:, j * 1024:j * 1024 + 512],
                                      AF.Identity, bias=nbv[:], scale=rsv[:])
                    scalar.activation(ln2dst[:, 512:1024],
                                      attnT[:, j * 1024 + 512:j * 1024 + 1024],
                                      AF.Identity, bias=nbv[:], scale=rsv[:]
                                      ).then_inc(cAP2.sem, 1)
                SE(_f)
                if j >= 1:
                    emit_add(stt - 1)
                if j == 3:
                    emit_add(stt)

        emit_C1(0)
        emit_C2(0)
        emit_C3(0)
        for tile in range(NT):
            emit_C4a(tile)
            if tile + 1 < NT:
                emit_C1(tile + 1)
                emit_C2(tile + 1, "head")
            emit_C4b(tile)
            if tile + 1 < NT:
                emit_C2(tile + 1, "rest")
                emit_C3(tile + 1)
        barrier()

        wffn_sb = poolA[:, 0:8192]
        h2_fm = poolA[:, 8192:12288]
        ffnT = poolA[:, 12288:16384]
        ut_sb = kvA[:, 0:1024]
        vfl_sb = kvA[:, 1024:2048]
        gwt_sb = bpad[:, 0:64]
        h2tb = [poolC[:, 0:1024], poolC[:, 1024:2048]]
        ytb = [poolC[:, 2048:3072], poolC[:, 3072:4096]]
        ffn_tmp = poolC[:, 4096:4608]
        w_sb = poolC[:, 4608:5120]
        gateb_sb = poolC[:, 5120:5632]
        ln4a = [poolC[:, 5632:6144], poolC[:, 6144:6656]]
        glog = poolC[0:8, 6656:7168]
        gfm = poolC[0:8, 7168:7680]

        cWF = C("cWF")
        cWF.inc(64)
        def _f(sync):
            src = bass.AP(gatewT_h.tensor, 0, [[8, 128], [128 * 8, 8], [1, 8]])
            sync.dma_start(gwt_sb[:].rearrange("p (c m) -> p c m", c=8),
                           src).then_inc(cWF.sem, 16)
            sync.wait_ge(cWF.sem, 16)
            src = bass.AP(ut_h.tensor, 0, [[128, 128], [128 * 128, 8], [1, 128]])
            sync.dma_start(ut_sb[:].rearrange("p (c m) -> p c m", c=8),
                           src).then_inc(cWF.sem, 16)
            sync.wait_ge(cWF.sem, 32)
            src = bass.AP(wffnT_h.tensor, 0, [[D, 128], [128 * D, 8], [1, D]])
            sync.dma_start(wffn_sb[:].rearrange("p (c m) -> p c m", c=8),
                           src).then_inc(cWF.sem, 16)
            sync.wait_ge(cWF.sem, 48)
            sync.dma_start(vfl_sb[:], vflat_h).then_inc(cWF.sem, 16)
        SP(_f)

        cLN3 = C("cLN3"); cH2T = C("cH2T"); cH2E = C("cH2E")
        cGM = C("cGM"); cGE = C("cGE"); cGT = C("cGT"); cSMX = C("cSMX")
        cGT2 = C("cGT2"); cGF = C("cGF"); cRGM = C("cRGM"); cRGE = C("cRGE")
        cTM = C("cTM"); cWM = C("cWM"); cFFM = C("cFFM"); cFFE = C("cFFE")
        cFFT = C("cFFT"); cFFTE = C("cFFTE"); cLN4 = C("cLN4"); cY = C("cY")
        cYSp = [C("cYSa"), C("cYSb")]

        def emit_D1(tile):
            def emit_h2evac(j):
                cH2E.inc()
                def _f(vector, j=j, tile=tile):
                    vector.wait_ge(cH2T.sem, tile * 4 + j + 1)
                    if j == 0 and tile > 0:
                        vector.wait_ge(cFFM.sem, 8 * tile)
                        vector.wait_ge(cGM.sem, tile)
                        vector.wait_ge(cTM.sem, tile)
                    pbase = 2 + (j % 2)
                    last = None
                    for c in range(NC):
                        b = pb[2 + c // 4]
                        last = vector.tensor_copy(
                            h2_fm[:, c * 512 + j * 128:c * 512 + j * 128 + 128],
                            b[:, (c % 4) * 128:(c % 4) * 128 + 128])
                    last.then_inc(cH2E.sem, 1)
                VE(_f)

            for j in range(4):
                stt = tile * 4 + j
                cNBc, vNB, rsv, nbv = ln_stats_chain(
                    x1[:, stt * 1024:stt * 1024 + 1024])
                vLN3 = cLN3.inc()
                def _f(scalar, stt=stt, j=j, cNBc=cNBc, vNB=vNB, rsv=rsv,
                       nbv=nbv, vLN3=vLN3):
                    scalar.wait_ge(cNBc.sem, vNB)
                    if vLN3 >= 3:
                        scalar.wait_ge(cH2T.sem, vLN3 - 2)
                    scalar.activation(h2tb[j % 2][:],
                                      x1[:, stt * 1024:stt * 1024 + 1024],
                                      AF.Identity, bias=nbv[:], scale=rsv[:]
                                      ).then_inc(cLN3.sem, 1)
                SE(_f)
                vH2T = cH2T.inc()
                def _f(tensor, j=j, vLN3=vLN3, vH2T=vH2T, tile=tile):
                    tensor.wait_ge(cLN3.sem, vLN3)
                    if j == 0 and tile > 0:
                        tensor.wait_ge(cFFTE.sem, 8 * tile)
                        tensor.wait_ge(cSMX.sem, 6 * tile)
                    if vH2T >= 2:
                        tensor.wait_ge(cH2E.sem, vH2T - 1)
                    last = None
                    for c in range(NC):
                        b = pb[2 + c // 4]
                        last = tensor.transpose(
                            b[:, (c % 4) * 128:(c % 4) * 128 + 128],
                            h2tb[j % 2][:, c * 128:c * 128 + 128], ident[:])
                    last.then_inc(cH2T.sem, 1)
                TE(_f)
                if j >= 1:
                    emit_h2evac(j - 1)
                if j == 3:
                    emit_h2evac(3)

        def emit_D2(tile):
            vGM = cGM.inc()
            def _f(tensor, tile=tile, vGM=vGM):
                tensor.wait_ge(cWF.sem, 16)
                tensor.wait_ge(cH2E.sem, 4 * (tile + 1))
                if tile > 0:
                    tensor.wait_ge(cGF.sem, tile)
                last = None
                for ic in range(8):
                    last = tensor.matmul(pb[4][0:8, :],
                                         gwt_sb[:, ic * 8:ic * 8 + 8],
                                         h2_fm[:, ic * 512:ic * 512 + 512],
                                         start=(ic == 0), stop=(ic == 7))
                last.then_inc(cGM.sem, 1)
            TE(_f)
            vGE = cGE.inc()
            def _f(scalar, vGM=vGM, tile=tile):
                scalar.wait_ge(cGM.sem, vGM)
                if tile > 0:
                    scalar.wait_ge(cGT.sem, tile)
                scalar.activation(glog[:, :], pb[4][0:8, :], AF.Identity,
                                  bias=0.0, scale=1.0).then_inc(cGE.sem, 1)
            SE(_f)
            vGT = cGT.inc()
            def _f(tensor, vGE=vGE):
                tensor.wait_ge(cGE.sem, vGE)
                last = None
                for j in range(4):
                    last = tensor.transpose(pb[2][:, j * 8:j * 8 + 8],
                                            glog[:, j * 128:j * 128 + 128],
                                            ident[0:8, 0:8])
                last.then_inc(cGT.sem, 1)
            TE(_f)
            vS1 = cSMX.inc()
            def _f(vector, vGT=vGT, tile=tile):
                vector.wait_ge(cGT.sem, vGT)
                if tile > 0:
                    vector.wait_ge(cGT2.sem, tile)
                in_ap = pb[2][:, 0:32].rearrange("p (j f) -> p j f", j=4)
                out_ap = gneg[:].rearrange("p (j o) -> p j o", o=1)
                vector.reduce_max(out_ap, in_ap, axis=AX, negate=True
                                  ).then_inc(cSMX.sem, 1)
            VE(_f)
            for j in range(4):
                cSMX.inc()
                def _f(scalar, j=j, vS1=vS1):
                    scalar.wait_ge(cSMX.sem, vS1)
                    scalar.activation(gtmp[:, j * 8:j * 8 + 8],
                                      pb[2][:, j * 8:j * 8 + 8], AF.Exp,
                                      bias=gneg[:, j:j + 1], scale=1.0
                                      ).then_inc(cSMX.sem, 1)
                SE(_f)
            cSMX.inc()
            def _f(vector, vS1=vS1, tile=tile):
                vector.wait_ge(cSMX.sem, vS1 + 4)
                in_ap = gtmp[:].rearrange("p (j f) -> p j f", j=4)
                out_ap = gsum[:].rearrange("p (j o) -> p j o", o=1)
                vector.reduce_sum(out_ap, in_ap, axis=AX)
                vector.drain()
                vector.reciprocal(gsum[:], gsum[:])
                vector.drain()
                gb = bass.AP(gsum, 0, [[4, 128], [1, 4], [0, 8]])
                vector.tensor_tensor(gtmp[:].rearrange("p (j f) -> p j f", j=4),
                                     gtmp[:].rearrange("p (j f) -> p j f", j=4),
                                     gb, op=OP.mult).then_inc(cSMX.sem, 1)
            VE(_f)
            vGT2 = cGT2.inc()
            def _f(tensor, tile=tile, vGT2=vGT2):
                tensor.wait_ge(cSMX.sem, tile * 6 + 6)
                last = None
                for j in range(4):
                    last = tensor.transpose(pb[4][0:8, j * 128:j * 128 + 128],
                                            gtmp[:, j * 8:j * 8 + 8],
                                            ident[:])
                last.then_inc(cGT2.sem, 1)
            TE(_f)
            vGF = cGF.inc()
            def _f(scalar, vGT2=vGT2, tile=tile):
                scalar.wait_ge(cGT2.sem, vGT2)
                if tile > 0:
                    scalar.wait_ge(cRGM.sem, tile)
                scalar.activation(gfm[:, :], pb[4][0:8, :], AF.Identity,
                                  bias=0.0, scale=1.0).then_inc(cGF.sem, 1)
            SE(_f)
            vRGM = cRGM.inc()
            def _f(tensor, vGF=vGF, tile=tile):
                tensor.wait_ge(cGF.sem, vGF)
                if tile > 0:
                    tensor.wait_ge(cRGE.sem, tile)
                tensor.matmul(pb[5][:], rgate[:], gfm[:, :], start=True,
                              stop=True).then_inc(cRGM.sem, 1)
            TE(_f)
            vRGE = cRGE.inc()
            def _f(scalar, vRGM=vRGM, tile=tile):
                scalar.wait_ge(cRGM.sem, vRGM)
                if tile > 0:
                    scalar.wait_ge(cWM.sem, tile)
                scalar.activation(gateb_sb[:], pb[5][:], AF.Identity,
                                  bias=0.0, scale=1.0).then_inc(cRGE.sem, 1)
            SE(_f)
            vTM = cTM.inc()
            def _f(tensor, vTM=vTM):
                if vTM == 1:
                    tensor.wait_ge(cWF.sem, 32)
                if vTM >= 2:
                    tensor.wait_ge(cWM.sem, vTM - 1)
                last = None
                for ic in range(8):
                    last = tensor.matmul(pb[6][:],
                                         ut_sb[:, ic * 128:ic * 128 + 128],
                                         h2_fm[:, ic * 512:ic * 512 + 512],
                                         start=(ic == 0), stop=(ic == 7))
                last.then_inc(cTM.sem, 1)
            TE(_f)
            vWM = cWM.inc()
            def _f(vector, vTM=vTM, vRGE=vRGE, tile=tile):
                vector.wait_ge(cTM.sem, vTM)
                vector.wait_ge(cRGE.sem, vRGE)
                if tile > 0:
                    vector.wait_ge(cFFM.sem, 8 * tile)
                vector.tensor_tensor(w_sb[:], pb[6][:], gateb_sb[:],
                                     op=OP.mult).then_inc(cWM.sem, 1)
            VE(_f)

        def emit_D3(tile):
            for pc in range(4):
                oc0, oc1 = 2 * pc, 2 * pc + 1
                g0 = tile * 8 + oc0
                cFFM.inc(2)
                def _f(tensor, pc=pc, oc0=oc0, oc1=oc1, g0=g0, tile=tile):
                    if pc == 0:
                        tensor.wait_ge(cH2E.sem, 4 * (tile + 1))
                        if tile == 0:
                            tensor.wait_ge(cWF.sem, 48)
                    if g0 >= 2:
                        tensor.wait_ge(cFFE.sem, g0)
                    for oc, bk in ((oc0, pb[0]), (oc1, pb[1])):
                        for ic in range(8):
                            tensor.matmul(
                                bk[:],
                                wffn_sb[:, ic * 1024 + oc * 128:
                                        ic * 1024 + oc * 128 + 128],
                                h2_fm[:, ic * 512:ic * 512 + 512],
                                start=(ic == 0), stop=False)
                    tensor.wait_ge(cWM.sem, tile + 1)
                    if tile == 0 and pc == 0:
                        tensor.wait_ge(cWF.sem, 64)
                    tensor.matmul(pb[0][:], vfl_sb[:, oc0 * 128:oc0 * 128 + 128],
                                  w_sb[:], start=False, stop=True
                                  ).then_inc(cFFM.sem, 1)
                    tensor.matmul(pb[1][:], vfl_sb[:, oc1 * 128:oc1 * 128 + 128],
                                  w_sb[:], start=False, stop=True
                                  ).then_inc(cFFM.sem, 1)
                TE(_f)
                for oc, bki in ((oc0, 0), (oc1, 1)):
                    g = tile * 8 + oc
                    vFFE = cFFE.inc()
                    def _f(vector, bki=bki, g=g, vFFE=vFFE):
                        vector.wait_ge(cFFM.sem, g + 1)
                        if vFFE >= 2:
                            vector.wait_ge(cFFT.sem, vFFE - 1)
                        vector.tensor_copy(ffn_tmp[:], pb[bki][:]
                                           ).then_inc(cFFE.sem, 1)
                    VE(_f)
                    vFFT = cFFT.inc()
                    def _f(tensor, oc=oc, vFFE=vFFE, vFFT=vFFT):
                        tensor.wait_ge(cFFE.sem, vFFE)
                        if vFFT >= 3:
                            tensor.wait_ge(cFFTE.sem, vFFT - 2)
                        b = pb[2 + oc % 2]
                        last = None
                        for j in range(4):
                            last = tensor.transpose(
                                b[:, j * 128:j * 128 + 128],
                                ffn_tmp[:, j * 128:j * 128 + 128], ident[:])
                        last.then_inc(cFFT.sem, 1)
                    TE(_f)
                    cFFTE.inc()
                    def _f(vector, oc=oc, vFFT=vFFT, tile=tile):
                        vector.wait_ge(cFFT.sem, vFFT)
                        if oc == 0 and tile > 0:
                            vector.wait_ge(cLN4.sem, 4 * tile)
                        b = pb[2 + oc % 2]
                        last = None
                        for j in range(4):
                            last = vector.tensor_copy(
                                ffnT[:, j * 1024 + oc * 128:
                                     j * 1024 + oc * 128 + 128],
                                b[:, j * 128:j * 128 + 128])
                        last.then_inc(cFFTE.sem, 1)
                    VE(_f)

        def emit_D4(tile):
            def emit_yout(stt):
                vY = cY.inc()
                cYSc = cYSp[stt % 2]
                def _f(vector, stt=stt, vY=vY, cYSc=cYSc):
                    vector.wait_ge(cLN4.sem, stt + 1)
                    if stt >= 2:
                        vector.wait_ge(cYSc.sem, 16 * (stt // 2))
                    vector.tensor_tensor(ytb[stt % 2][:, 0:512],
                                         x1[:, stt * 1024:stt * 1024 + 512],
                                         ln4a[0][:], op=OP.add)
                    vector.tensor_tensor(ytb[stt % 2][:, 512:1024],
                                         x1[:, stt * 1024 + 512:
                                            stt * 1024 + 1024],
                                         ln4a[1][:], op=OP.add
                                         ).then_inc(cY.sem, 1)
                VE(_f)
                cYSc.inc(16)
                def _f(gpsimd, stt=stt, vY=vY, cYSc=cYSc):
                    gpsimd.wait_ge(cY.sem, vY)
                    gpsimd.dma_start(y_h[stt * 128:stt * 128 + 128, :],
                                     ytb[stt % 2][:]).then_inc(cYSc.sem, 16)
                GP(_f)

            for j in range(4):
                stt = tile * 4 + j
                def _f(vector, tile=tile):
                    vector.wait_ge(cFFTE.sem, tile * 8 + 8)
                VE(_f)
                cNBc, vNB, rsv, nbv = ln_stats_chain(
                    ffnT[:, j * 1024:j * 1024 + 1024])
                vLN4 = cLN4.inc()
                def _f(scalar, j=j, cNBc=cNBc, vNB=vNB, rsv=rsv, nbv=nbv,
                       stt=stt):
                    scalar.wait_ge(cNBc.sem, vNB)
                    if stt >= 1:
                        scalar.wait_ge(cY.sem, stt)
                    scalar.activation(ln4a[0][:],
                                      ffnT[:, j * 1024:j * 1024 + 512],
                                      AF.Identity, bias=nbv[:], scale=rsv[:])
                    scalar.activation(ln4a[1][:],
                                      ffnT[:, j * 1024 + 512:j * 1024 + 1024],
                                      AF.Identity, bias=nbv[:], scale=rsv[:]
                                      ).then_inc(cLN4.sem, 1)
                SE(_f)
                if j >= 1:
                    emit_yout(stt - 1)
                if j == 3:
                    emit_yout(stt)

        emit_D1(0)
        emit_D2(0)
        for tile in range(NT):
            emit_D3(tile)
            if tile + 1 < NT:
                emit_D1(tile + 1)
                emit_D2(tile + 1)
            emit_D4(tile)

        vYSa_f, vYSb_f = cYSp[0].n, cYSp[1].n
        def _f(gpsimd, vYSa_f=vYSa_f, vYSb_f=vYSb_f):
            gpsimd.wait_ge(cYSp[0].sem, vYSa_f)
            gpsimd.wait_ge(cYSp[1].sem, vYSb_f)
        GP(_f)
        def _f(sync, vYSa_f=vYSa_f, vYSb_f=vYSb_f):
            sync.wait_ge(cYSp[0].sem, vYSa_f)
            sync.wait_ge(cYSp[1].sem, vYSb_f)
        SP(_f)

        with nc.Block() as block:
            def run_for(ek):
                def body(eng):
                    for e, fn in plan:
                        if e == ek:
                            fn(eng)
                return body
            block.sync(run_for("sync"))
            block.tensor(run_for("tensor"))
            block.vector(run_for("vector"))
            block.scalar(run_for("scalar"))
            block.gpsimd(run_for("gpsimd"))

    return nc


def _host_prep(L0_attn_x, L0_attn_W_qkv, L0_attn_W_out, L0_attn_proj,
               L0_ffn_weight, L0_ffn_gate_w, L0_ffn_lora_U, L0_ffn_lora_V):
    pmT = np.transpose(np.asarray(L0_attn_proj), (0, 2, 1))
    pm2 = np.concatenate([pmT, pmT], axis=1)
    pm2 = np.ascontiguousarray(
        np.transpose(pm2, (1, 0, 2)).reshape(128, H * 128))
    shared = {
        "wqkv": np.ascontiguousarray(np.asarray(L0_attn_W_qkv, np.float32)),
        "wout": np.ascontiguousarray(np.asarray(L0_attn_W_out, np.float32)),
        "pm2": pm2.astype(np.float32),
        "wffnT": np.ascontiguousarray(np.asarray(L0_ffn_weight, np.float32).T),
        "gatewT": np.ascontiguousarray(np.asarray(L0_ffn_gate_w, np.float32).T),
        "ut": np.ascontiguousarray(
            np.asarray(L0_ffn_lora_U, np.float32).reshape(128, D).T),
        "vflat": np.ascontiguousarray(
            np.transpose(np.asarray(L0_ffn_lora_V, np.float32),
                         (0, 2, 1)).reshape(128, D)),
    }
    x = np.asarray(L0_attn_x, np.float32)
    return [dict(shared, x=np.ascontiguousarray(x[b])) for b in range(NCORES)]


def kernel(**inputs):
    in_maps = _host_prep(**inputs)
    nc = build_nc()
    res = run_bass_kernel_spmd(nc, in_maps, core_ids=list(range(NCORES)))
    return np.stack([np.asarray(r["y"], dtype=np.float32)
                     for r in res.results], axis=0)


# version 46
# speedup vs baseline: 1.0399x; 1.0399x over previous
import numpy as np
import concourse.bass as bass
import concourse.mybir as mybir
from concourse.bass_utils import run_bass_kernel_spmd

F32 = mybir.dt.float32
AF = mybir.ActivationFunctionType
OP = mybir.AluOpType
AX = mybir.AxisListType.X

NCORES = 8
N = 2048
D = 1024
H = 16
NT, T = 4, 512
NST = 16
NC = 8
LN_EPS = 1e-5
LN256EPS = float(np.log(256.0 * 1e-6))


class Chan:
    __slots__ = ("sem", "n", "name")

    def __init__(self, name, sem):
        self.name, self.sem, self.n = name, sem, 0

    def inc(self, k=1):
        self.n += k
        return self.n


def build_nc():
    nc = bass.Bass("TRN2", target_bir_lowering=False)

    x_h = nc.dram_tensor("x", [N, D], F32, kind="ExternalInput").ap()
    wqkv_h = nc.dram_tensor("wqkv", [D, 3 * D], F32, kind="ExternalInput").ap()
    wout_h = nc.dram_tensor("wout", [D, D], F32, kind="ExternalInput").ap()
    pm2_h = nc.dram_tensor("pm2", [128, H * 128], F32, kind="ExternalInput").ap()
    wffnT_h = nc.dram_tensor("wffnT", [D, D], F32, kind="ExternalInput").ap()
    gatewT_h = nc.dram_tensor("gatewT", [D, 8], F32, kind="ExternalInput").ap()
    ut_h = nc.dram_tensor("ut", [D, 128], F32, kind="ExternalInput").ap()
    vflat_h = nc.dram_tensor("vflat", [128, D], F32, kind="ExternalInput").ap()
    y_h = nc.dram_tensor("y", [N, D], F32, kind="ExternalOutput").ap()
    qs_h = nc.dram_tensor("q_scratch", [NC, 128, N], F32, kind="Internal")
    vs_h = nc.dram_tensor("v_scratch", [N, D], F32, kind="Internal").ap()

    ident_np = np.eye(128, dtype=np.float32)
    d2h_np = np.zeros((128, 128), dtype=np.float32)
    for c in range(NC):
        for p in range(128):
            d2h_np[p, c * 16 + (2 * c + p // 64)] = 1.0
    rgate_np = np.zeros((8, 128), dtype=np.float32)
    for er in range(128):
        rgate_np[er // 16, er] = 1.0 / 16.0
    epair_np = np.zeros((16, 8 * 128), dtype=np.float32)
    for c in range(8):
        for m in range(128):
            epair_np[2 * c + m // 64, c * 128 + m] = 1.0
    cst_vals = [0.0, 1.0, LN_EPS, LN256EPS, 0.125, -0.125, 0.5, -0.5, -1.0]
    cst_np = np.tile(np.array(cst_vals, np.float32)[None, :], (128, 1))

    plan = []

    def SP(fn): plan.append(("sync", fn))
    def GP(fn): plan.append(("gpsimd", fn))
    def TE(fn): plan.append(("tensor", fn))

    def VE(fn): plan.append(("vector", fn))
    def SE(fn): plan.append(("scalar", fn))

    from contextlib import ExitStack
    st_ctx = ExitStack()

    def sbt(name, shape, dtype=F32):
        return st_ctx.enter_context(nc.sbuf_tensor(name, shape, dtype))

    def pst(name, shape, dtype=F32):
        return st_ctx.enter_context(nc.psum_tensor(name, shape, dtype))

    chans = {}

    def C(name):
        if name not in chans:
            sem = st_ctx.enter_context(nc.semaphore(name))
            chans[name] = Chan(name, sem)
        return chans[name]

    with st_ctx:
        poolA = sbt("poolA", [128, 16640])
        poolB = sbt("poolB", [128, 16384])
        poolC = sbt("poolC", [128, 8192])
        pm2 = sbt("pm2s", [128, H * 128])
        kvA = sbt("kvA", [128, H * 130])
        bpad = sbt("bpad", [128, 512])
        gk = sbt("gk", [128, NST * 16])
        ident = sbt("ident", [128, 128])
        d2h = sbt("d2h", [128, 128])
        rgate = sbt("rgate", [8, 128])
        epair = sbt("epair", [16, 8 * 128])
        cst = sbt("cst", [128, len(cst_vals)])
        stats12s = [sbt(f"stats12{i}", [128, 12]) for i in range(3)]
        mvs = [sbt(f"mv{i}", [128, 2]) for i in range(3)]
        lnvs = [sbt(f"lnv{i}", [128, 1]) for i in range(3)]
        rsbs = [sbt(f"rsb{i}", [128, 1]) for i in range(3)]
        nbs = [sbt(f"nb{i}", [128, 1]) for i in range(3)]
        smax = sbt("smax", [128, 1])
        stabk_raw = sbt("stabk_raw", [128, 16])
        stk_t = sbt("stk_t", [16, 2])
        gtmp = sbt("gtmp", [128, 32])
        gneg = sbt("gneg", [128, 4])
        gsum = sbt("gsum", [128, 4])

        ident_t = nc.inline_tensor(ident_np, "ident_c")
        d2h_t = nc.inline_tensor(d2h_np, "d2h_c")
        rgate_t = nc.inline_tensor(rgate_np, "rgate_c")
        epair_t = nc.inline_tensor(epair_np, "epair_c")
        cst_t = nc.inline_tensor(cst_np, "cst_c")
        for i, v in enumerate(cst_vals):
            nc.const_aps.register(F32, float(v), cst[:, i:i + 1])

        pb = [pst(f"pb{i}", [128, 512]) for i in range(8)]

        bar = C("bar")
        nbar = [0]

        def barrier():
            nbar[0] += 1
            tgt = 5 * nbar[0]
            bar.n = tgt
            for em, dr in ((SP, False), (TE, True), (VE, True), (SE, True),
                           (GP, False)):
                def _f(eng, tgt=tgt, dr=dr):
                    if dr:
                        eng.drain().then_inc(bar.sem, 1)
                    else:
                        eng.nop().then_inc(bar.sem, 1)
                    eng.wait_ge(bar.sem, tgt)
                em(_f)

        cMVs = [C("cMV0"), C("cMV1"), C("cMV2")]
        cRSs = [C("cRS0"), C("cRS1"), C("cRS2")]
        cNBs = [C("cNB0"), C("cNB1"), C("cNB2")]
        ln_i = [0]

        def ln_stats_chain(src_ap):
            s = ln_i[0] % 3
            ln_i[0] += 1
            cMVc, cRSc, cNBc = cMVs[s], cRSs[s], cNBs[s]
            st12, mvv, lnvv, rsv, nbv = (stats12s[s], mvs[s], lnvs[s],
                                         rsbs[s], nbs[s])
            prior_rs = cRSc.n
            vMV = cMVc.inc()
            def _f(vector, src_ap=src_ap, prior_rs=prior_rs, st12=st12,
                   mvv=mvv, cRSc=cRSc, cMVc=cMVc):
                if prior_rs:
                    vector.wait_ge(cRSc.sem, prior_rs)
                vector.bn_stats(st12[:, 0:6], src_ap[:, 0:512])
                vector.drain()
                vector.bn_stats(st12[:, 6:12], src_ap[:, 512:1024])
                vector.drain()
                vector.bn_aggr(mvv[:], st12[:]).then_inc(cMVc.sem, 1)
            VE(_f)
            vRS = cRSc.inc()
            def _f(scalar, vMV=vMV, mvv=mvv, lnvv=lnvv, rsv=rsv, cMVc=cMVc,
                   cRSc=cRSc):
                scalar.wait_ge(cMVc.sem, vMV)
                scalar.activation(lnvv[:], mvv[:, 1:2], AF.Ln, bias=LN_EPS,
                                  scale=1.0)
                scalar.drain()
                scalar.activation(rsv[:], lnvv[:], AF.Exp, bias=0.0,
                                  scale=-0.5).then_inc(cRSc.sem, 1)
            SE(_f)
            vNB = cNBc.inc()
            def _f(vector, vRS=vRS, mvv=mvv, rsv=rsv, nbv=nbv, cRSc=cRSc,
                   cNBc=cNBc):
                vector.wait_ge(cRSc.sem, vRS)
                vector.tensor_scalar(nbv[:], mvv[:, 0:1], rsv[:], -1.0,
                                     op0=OP.mult, op1=OP.mult
                                     ).then_inc(cNBc.sem, 1)
            VE(_f)
            return cNBc, vNB, rsv, nbv

        cLD = C("cLD")
        def _f(sync):
            sync.dma_start(ident[:], ident_t.ap()).then_inc(cLD.sem, 16)
            sync.dma_start(d2h[:], d2h_t.ap()).then_inc(cLD.sem, 16)
            sync.dma_start(rgate[:], rgate_t.ap()).then_inc(cLD.sem, 16)
            sync.dma_start(epair[:], epair_t.ap()).then_inc(cLD.sem, 16)
            sync.dma_start(cst[:], cst_t.ap()).then_inc(cLD.sem, 16)
            sync.dma_start(pm2[:], pm2_h).then_inc(cLD.sem, 16)
            sync.wait_ge(cLD.sem, 96)
        SP(_f)
        cLD.inc(96)
        barrier()

        h1 = poolA
        k_fm = poolB
        wqb = [poolC[:, 0:1024], poolC[:, 1024:2048]]
        xb = [poolC[:, 2048:3072], poolC[:, 3072:4096]]
        h1tb = [poolC[:, 4096:5120], poolC[:, 5120:6144]]
        qob = [poolC[:, 6144:6656], poolC[:, 6656:7168]]
        vtmp = poolC[:, 7168:7680]
        vtout = poolC[:, 7680:8192]

        cXp = [C("cXa"), C("cXb")]
        cAPP = C("cAPP"); cH1T = C("cH1T"); cH1V = C("cH1V")

        def emit_a0_evac(st):
            cH1V.inc()
            def _f(vector, st=st):
                vector.wait_ge(cH1T.sem, st + 1)
                pbase = (st % 2) * 2
                last = None
                for c in range(NC):
                    b = pb[pbase + c // 4]
                    last = vector.tensor_copy(
                        h1[:, c * 2048 + st * 128:c * 2048 + st * 128 + 128],
                        b[:, (c % 4) * 128:(c % 4) * 128 + 128])
                last.then_inc(cH1V.sem, 1)
            VE(_f)

        for st in range(NST):
            cXc = cXp[st % 2]
            vX = cXc.inc(16)
            def _f(sync, st=st, cXc=cXc):
                if st >= 2:
                    sync.wait_ge(cAPP.sem, st - 1)
                sync.dma_start(xb[st % 2][:], x_h[st * 128:(st + 1) * 128, :]
                               ).then_inc(cXc.sem, 16)
            SP(_f)
            def _f(vector, cXc=cXc, vX=vX):
                vector.wait_ge(cXc.sem, vX)
            VE(_f)
            cNBc, vNB, rsv, nbv = ln_stats_chain(xb[st % 2][:])
            vAPP = cAPP.inc()
            def _f(scalar, st=st, cNBc=cNBc, vNB=vNB, rsv=rsv, nbv=nbv):
                scalar.wait_ge(cNBc.sem, vNB)
                if st >= 2:
                    scalar.wait_ge(cH1T.sem, st - 1)
                scalar.activation(h1tb[st % 2][:], xb[st % 2][:], AF.Identity,
                                  bias=nbv[:], scale=rsv[:]).then_inc(cAPP.sem, 1)
            SE(_f)
            cH1T.inc()
            def _f(tensor, st=st, vAPP=vAPP):
                tensor.wait_ge(cAPP.sem, vAPP)
                if st >= 2:
                    tensor.wait_ge(cH1V.sem, st - 1)
                pbase = (st % 2) * 2
                last = None
                for c in range(NC):
                    b = pb[pbase + c // 4]
                    last = tensor.transpose(
                        b[:, (c % 4) * 128:(c % 4) * 128 + 128],
                        h1tb[st % 2][:, c * 128:c * 128 + 128], ident[:])
                last.then_inc(cH1T.sem, 1)
            TE(_f)
            if st >= 1:
                emit_a0_evac(st - 1)
        emit_a0_evac(NST - 1)
        barrier()

        cWQp = [C("cWQa"), C("cWQb")]
        cQKM = C("cQKM"); cQKE = C("cQKE")
        cQSp = [C("cQSa"), C("cQSb")]
        cVTa = C("cVTa"); cVT2 = C("cVT2"); cVTb = C("cVTb"); cVS = C("cVS")
        qg = [0]
        vg = [0]
        for oc in range(24):
            cWQc = cWQp[oc % 2]
            vWQ = cWQc.inc(16)
            def _f(sync, oc=oc, cWQc=cWQc):
                if oc >= 2:
                    sync.wait_ge(cQKM.sem, (oc - 1) * NT)
                src = bass.AP(wqkv_h.tensor, oc * 128,
                              [[3 * D, 128], [128 * 3 * D, 8], [1, 128]])
                sync.dma_start(
                    wqb[oc % 2][:].rearrange("p (c m) -> p c m", c=8),
                    src).then_inc(cWQc.sem, 16)
            SP(_f)
            for tile in range(NT):
                g = oc * NT + tile
                bank = pb[g % 2]
                vQKM = cQKM.inc()
                def _f(tensor, oc=oc, tile=tile, g=g, bank=bank, vWQ=vWQ,
                       cWQc=cWQc):
                    tensor.wait_ge(cWQc.sem, vWQ)
                    if g >= 2:
                        tensor.wait_ge(cQKE.sem, g - 1)
                    last = None
                    for ic in range(8):
                        last = tensor.matmul(
                            bank[:, :],
                            wqb[oc % 2][:, ic * 128:ic * 128 + 128],
                            h1[:, ic * 2048 + tile * 512:
                               ic * 2048 + tile * 512 + 512],
                            start=(ic == 0), stop=(ic == 7))
                    last.then_inc(cQKM.sem, 1)
                TE(_f)
                if oc < 8:
                    q = qg[0]; qg[0] += 1
                    cQSc = cQSp[q % 2]
                    vQKE = cQKE.inc()
                    def _f(vector, q=q, bank=bank, vQKM=vQKM, cQSc=cQSc):
                        vector.wait_ge(cQKM.sem, vQKM)
                        if q >= 2:
                            vector.wait_ge(cQSc.sem, 16 * (q // 2))
                        vector.tensor_copy(qob[q % 2][:], bank[:]
                                           ).then_inc(cQKE.sem, 1)
                    VE(_f)
                    cQSc.inc(16)
                    def _f(gpsimd, oc=oc, tile=tile, q=q, vQKE=vQKE,
                           cQSc=cQSc):
                        gpsimd.wait_ge(cQKE.sem, vQKE)
                        gpsimd.dma_start(
                            qs_h.ap()[oc, :, tile * 512:tile * 512 + 512],
                            qob[q % 2][:]).then_inc(cQSc.sem, 16)
                    GP(_f)
                elif oc < 16:
                    cQKE.inc()
                    def _f(vector, oc=oc, tile=tile, bank=bank, vQKM=vQKM):
                        vector.wait_ge(cQKM.sem, vQKM)
                        c = oc - 8
                        vector.tensor_copy(
                            k_fm[:, c * 2048 + tile * 512:
                                 c * 2048 + tile * 512 + 512],
                            bank[:]).then_inc(cQKE.sem, 1)
                    VE(_f)
                else:
                    v = vg[0]; vg[0] += 1
                    vVTa = cVTa.inc()
                    cQKE.inc()
                    def _f(vector, v=v, bank=bank, vQKM=vQKM):
                        vector.wait_ge(cQKM.sem, vQKM)
                        if v >= 1:
                            vector.wait_ge(cVT2.sem, v)
                        vector.tensor_copy(vtmp[:], bank[:]).then_inc(cVTa.sem, 1)
                        vector.drain().then_inc(cQKE.sem, 1)
                    VE(_f)
                    vVT2 = cVT2.inc()
                    def _f(tensor, v=v, vVTa=vVTa):
                        tensor.wait_ge(cVTa.sem, vVTa)
                        if v >= 1:
                            tensor.wait_ge(cVTb.sem, v)
                        last = None
                        for j in range(4):
                            last = tensor.transpose(
                                pb[7][:, j * 128:j * 128 + 128],
                                vtmp[:, j * 128:j * 128 + 128], ident[:])
                        last.then_inc(cVT2.sem, 1)
                    TE(_f)
                    vVTb = cVTb.inc()
                    def _f(vector, v=v, vVT2=vVT2):
                        vector.wait_ge(cVT2.sem, vVT2)
                        if v >= 1:
                            vector.wait_ge(cVS.sem, 64 * v)
                        vector.tensor_copy(vtout[:], pb[7][:]
                                           ).then_inc(cVTb.sem, 1)
                    VE(_f)
                    cVS.inc(64)
                    def _f(gpsimd, oc=oc, tile=tile, vVTb=vVTb):
                        gpsimd.wait_ge(cVTb.sem, vVTb)
                        for j in range(4):
                            gpsimd.dma_start(
                                vs_h[tile * 512 + j * 128:
                                     tile * 512 + j * 128 + 128,
                                     (oc - 16) * 128:(oc - 16) * 128 + 128],
                                vtout[:, j * 128:j * 128 + 128]
                            ).then_inc(cVS.sem, 16)
                    GP(_f)
        barrier()

        kf_sb = poolC
        vld = [poolC[:, 4096:5120], poolC[:, 5120:6144]]
        kvT = poolC[:, 6144:6656]
        k2b = [poolC[:, 6656:7168], poolC[:, 7168:7680]]
        dk_sb = poolC[:, 7680:8192]

        cK2 = C("cK2"); cDKM = C("cDKM"); cDKE = C("cDKE"); cDKT = C("cDKT")
        cGKE = C("cGKE")
        vQSa_0, vQSb_0, vVS_0 = cQSp[0].n, cQSp[1].n, cVS.n
        v_aug = poolA
        def _vdst(st):
            return bass.AP(poolA, st * 1040, [[16640, 128], [65, 16], [1, 64]])
        def _vsrc(st):
            return vs_h[st * 128:(st + 1) * 128, :].rearrange(
                "p (h j) -> p h j", h=16)
        cVLp = [C("cVLa"), C("cVLb")]
        dma_wait = {}
        def _next(chan, st, first=False):
            pre = None if first else chan.n
            v = chan.inc(16)
            dma_wait[st] = (chan, v)
            return (st, pre, chan)
        def _vhost_off(st):
            return st * 1024 if st < 4 else 4096 + (st - 4) * 1024
        def _emit_issue(eng, item):
            st, pre, chan = item
            if pre is not None:
                eng.wait_ge(chan.sem, pre)
            if st < 6:
                dst = poolC[:, _vhost_off(st):_vhost_off(st) + 1024]
                eng.dma_start(dst, vs_h[st * 128:(st + 1) * 128, :]
                              ).then_inc(chan.sem, 16)
            else:
                eng.dma_start(_vdst(st), _vsrc(st)).then_inc(chan.sem, 16)
        se_plan = [_next(cVT2, st, True) for st in (1, 3, 5)]
        sp_plan = ([_next(cVTa, st, True) for st in (0, 2, 4)]
                   + [_next(cVLp[0], st, True) for st in (6, 8, 10, 12, 14)])
        gp_plan = ([_next(cQSp[1], st, True) for st in (7, 11, 15)]
                   + [_next(cQSp[0], st, True) for st in (9, 13)])
        vPreA = cVTa.n
        vPreB = cVT2.n
        vStrA = cVLp[0].n
        vStrB = cQSp[1].n
        vStrC = cQSp[0].n
        def _f(scalar, a=vQSa_0, b=vQSb_0, v=vVS_0, plan=se_plan):
            scalar.wait_ge(cQSp[0].sem, a)
            scalar.wait_ge(cQSp[1].sem, b)
            scalar.wait_ge(cVS.sem, v)
            for item in plan:
                _emit_issue(scalar, item)
        SE(_f)
        def _f(sync, a=vQSa_0, b=vQSb_0, v=vVS_0, plan=sp_plan):
            sync.wait_ge(cQSp[0].sem, a)
            sync.wait_ge(cQSp[1].sem, b)
            sync.wait_ge(cVS.sem, v)
            for item in plan:
                _emit_issue(sync, item)
        SP(_f)
        def _f(gpsimd, a=vQSa_0, b=vQSb_0, v=vVS_0, plan=gp_plan):
            gpsimd.wait_ge(cVS.sem, v)
            gpsimd.wait_ge(cQSp[0].sem, a)
            gpsimd.wait_ge(cQSp[1].sem, b)
            for item in plan:
                _emit_issue(gpsimd, item)
        GP(_f)
        def _f(vector):
            vector.memset(bpad[:], 0.0)
        VE(_f)
        for tile in range(NT):
            for c in range(NC):
                vK2 = cK2.inc()
                def _f(scalar, tile=tile, c=c, vK2=vK2):
                    if vK2 >= 3:
                        scalar.wait_ge(cDKM.sem, vK2 - 2)
                    scalar.activation(k2b[c % 2][:],
                                      k_fm[:, c * 2048 + tile * 512:
                                           c * 2048 + tile * 512 + 512],
                                      AF.Square, bias=0.0, scale=1.0
                                      ).then_inc(cK2.sem, 1)
                SE(_f)
                vDKM = cDKM.inc()
                def _f(tensor, tile=tile, c=c, vK2=vK2):
                    tensor.wait_ge(cK2.sem, vK2)
                    if tile > 0 and c == 0:
                        tensor.wait_ge(cDKE.sem, tile)
                    tensor.matmul(pb[4][0:16, :], d2h[:, c * 16:c * 16 + 16],
                                  k2b[c % 2][:], start=(c == 0), stop=(c == 7)
                                  ).then_inc(cDKM.sem, 1)
                TE(_f)
            vDKE = cDKE.inc()
            def _f(scalar, tile=tile, vDKE=vDKE):
                scalar.wait_ge(cDKM.sem, tile * 8 + 8)
                if tile > 0:
                    scalar.wait_ge(cDKT.sem, tile)
                scalar.activation(dk_sb[0:16, :], pb[4][0:16, :], AF.Identity,
                                  bias=0.0, scale=1.0).then_inc(cDKE.sem, 1)
            SE(_f)
            vDKT = cDKT.inc()
            def _f(tensor, tile=tile, vDKE=vDKE):
                tensor.wait_ge(cDKE.sem, vDKE)
                if tile > 0:
                    tensor.wait_ge(cGKE.sem, tile)
                last = None
                for j in range(4):
                    last = tensor.transpose(pb[2][:, j * 16:j * 16 + 16],
                                            dk_sb[0:16, j * 128:j * 128 + 128],
                                            ident[0:16, 0:16])
                last.then_inc(cDKT.sem, 1)
            TE(_f)
            cGKE.inc()
            def _f(scalar, tile=tile, vDKT=vDKT):
                scalar.wait_ge(cDKT.sem, vDKT)
                scalar.activation(gk[:, tile * 64:tile * 64 + 64],
                                  pb[2][:, 0:64], AF.Exp, bias=0.0,
                                  scale=-0.5).then_inc(cGKE.sem, 1)
            SE(_f)

        cVA = C("cVA")
        for st in range(NST):
            cVA.inc()
            def _f(vector, st=st):
                if st < 6:
                    vector.wait_ge(cVTa.sem, vPreA)
                    vector.wait_ge(cVT2.sem, vPreB)
                else:
                    vector.wait_ge(cVLp[0].sem, vStrA)
                    vector.wait_ge(cQSp[1].sem, vStrB)
                    vector.wait_ge(cQSp[0].sem, vStrC)
                vector.wait_ge(cGKE.sem, st // 4 + 1)
                out_ap = bass.AP(poolA, st * 1040,
                                 [[16640, 128], [65, 16], [1, 64]])
                in_ap = (bass.AP(poolC, _vhost_off(st),
                                 [[8192, 128], [64, 16], [1, 64]])
                         if st < 6 else out_ap)
                gk_b = bass.AP(gk, st * 16, [[256, 128], [1, 16], [0, 64]])
                vector.tensor_tensor(out_ap, in_ap, gk_b, op=OP.mult)
                vector.drain()
                gcols = bass.AP(poolA, st * 1040 + 64, [[16640, 128], [65, 16]])
                vector.tensor_copy(gcols, gk[:, st * 16:st * 16 + 16]
                                   ).then_inc(cVA.sem, 1)
            VE(_f)
        barrier()

        cUK = C("cUK"); cKFe = C("cKFe"); cKFr = C("cKFr")
        cSTB = C("cSTB"); cKVP = C("cKVP")
        cKVE = C("cKVE"); cKVT = C("cKVT"); cKVAE = C("cKVAE")

        def emit_kv_part(h, stg):
            kvbank = pb[5 + h % 2]
            vKVP = cKVP.inc()
            def _f(tensor, h=h, stg=stg, kvbank=kvbank):
                tensor.wait_ge(cKFe.sem, 8 * h + 2 * (stg + 1))
                if stg == 0 and h >= 2:
                    tensor.wait_ge(cKVE.sem, h - 1)
                last = None
                for j in range(4):
                    stt = stg * 4 + j
                    last = tensor.matmul(kvbank[0:65, 0:256],
                                         v_aug[:, stt * 1040 + h * 65:
                                               stt * 1040 + h * 65 + 65],
                                         kf_sb[:, stt * 256:stt * 256 + 256],
                                         start=(stt == 0),
                                         stop=(stt == NST - 1))
                last.then_inc(cKVP.sem, 1)
            TE(_f)

        def emit_kv_evac(h):
            kvbank = pb[5 + h % 2]
            vKVE = cKVE.inc()
            def _f(scalar, h=h, kvbank=kvbank, vKVE=vKVE):
                scalar.wait_ge(cKVP.sem, 4 * (h + 1))
                if vKVE >= 2:
                    scalar.wait_ge(cKVT.sem, vKVE - 1)
                scalar.activation(kvT[0:65, 0:256], kvbank[0:65, 0:256],
                                  AF.Identity, bias=0.0, scale=1.0
                                  ).then_inc(cKVE.sem, 1)
            SE(_f)
            vKVT = cKVT.inc()
            def _f(tensor, vKVE=vKVE, vKVT=vKVT):
                tensor.wait_ge(cKVE.sem, vKVE)
                if vKVT >= 2:
                    tensor.wait_ge(cKVAE.sem, vKVT - 1)
                last = None
                for half in range(2):
                    last = tensor.transpose(pb[7][:, half * 65:half * 65 + 65],
                                            kvT[0:65, half * 128:half * 128 + 128],
                                            ident[0:65, 0:65])
                last.then_inc(cKVT.sem, 1)
            TE(_f)
            cKVAE.inc()
            def _f(vector, h=h, vKVT=vKVT):
                vector.wait_ge(cKVT.sem, vKVT)
                vector.tensor_copy(kvA[:, h * 130:h * 130 + 130],
                                   pb[7][:, 0:130])
                last = None
                for half in range(2):
                    last = vector.tensor_copy(
                        bpad[:, (2 * h + half) * 16 + h:
                             (2 * h + half) * 16 + h + 1],
                        pb[7][:, half * 65 + 64:half * 65 + 65])
                last.then_inc(cKVAE.sem, 1)
            VE(_f)

        for h in range(H):
            c = h // 2
            ro = (h % 2) * 64
            for stg in range(4):
                bank = pb[2 + stg % 2]
                vUK = cUK.inc()
                def _f(tensor, h=h, c=c, ro=ro, stg=stg, bank=bank, vUK=vUK):
                    if vUK >= 3:
                        tensor.wait_ge(cKFe.sem, 2 * (vUK - 2))
                        tensor.wait_ge(cSTB.sem, vUK - 2)
                    last = None
                    for j in range(4):
                        stt = stg * 4 + j
                        last = tensor.matmul(
                            bank[:, j * 128:j * 128 + 128],
                            k_fm[ro:ro + 64,
                                 c * 2048 + stt * 128:c * 2048 + stt * 128 + 128],
                            pm2[ro:ro + 64, h * 128:h * 128 + 128],
                            start=True, stop=True)
                    last.then_inc(cUK.sem, 1)
                TE(_f)
                vKF = cKFe.inc(2)
                def _f(scalar, h=h, stg=stg, bank=bank, vUK=vUK):
                    scalar.wait_ge(cUK.sem, vUK)
                    if h == 0:
                        scalar.wait_ge(cVA.sem, stg + 1)
                    if h >= 1:
                        scalar.wait_ge(cKVP.sem, 4 * (h - 1) + stg + 1)
                    for sgn in range(2):
                        out_ap = bass.AP(poolC, stg * 4 * 256 + sgn * 128,
                                         [[8192, 128], [256, 4], [1, 128]])
                        in_ap = bank[:].rearrange("p (j f) -> p j f", j=4)
                        scalar.activation(out_ap, in_ap, AF.Exp, bias=0.0,
                                          scale=(0.125 if sgn == 0 else -0.125)
                                          ).then_inc(cKFe.sem, 1)
                SE(_f)
                cSTB.inc()
                def _f(vector, h=h, stg=stg, bank=bank, vUK=vUK, vKF=vKF):
                    vector.wait_ge(cUK.sem, vUK)
                    vector.wait_ge(cKFe.sem, vKF)
                    vector.drain()
                    if stg == 0:
                        vector.reduce_max(stabk_raw[:, h:h + 1], bank[:],
                                          axis=AX).then_inc(cSTB.sem, 1)
                    else:
                        vector.reduce_max(smax[:], bank[:], axis=AX)
                        vector.drain()
                        vector.tensor_tensor(stabk_raw[:, h:h + 1],
                                             stabk_raw[:, h:h + 1], smax[:],
                                             op=OP.max).then_inc(cSTB.sem, 1)
                VE(_f)
                if stg >= 1:
                    emit_kv_part(h, stg - 1)
                elif h >= 1:
                    emit_kv_part(h - 1, 3)
                    emit_kv_evac(h - 1)
        emit_kv_part(H - 1, 3)
        emit_kv_evac(H - 1)

        cSK = C("cSK"); cBP = C("cBP")
        cSK.inc()
        def _f(tensor):
            tensor.wait_ge(cSTB.sem, 64)
            tensor.transpose(pb[2][0:16, 0:128], stabk_raw[:], ident[:]
                             ).then_inc(cSK.sem, 1)
        TE(_f)
        cBP.inc()
        def _f(vector):
            vector.wait_ge(cSK.sem, 1)
            vector.reduce_max(stk_t[:, 0:1], pb[2][0:16, 0:128], axis=AX)
            vector.drain()
            vector.tensor_scalar(stk_t[:, 1:2], stk_t[:, 0:1], 0.125, LN256EPS,
                                 op0=OP.mult, op1=OP.add).then_inc(cBP.sem, 1)
        VE(_f)

        wout_sb = poolA[:, 0:8192]
        q_sb = poolA[:, 8192:12288]
        cWO = C("cWO"); cWO2 = C("cWO2")
        cQLp = [C("cQLa"), C("cQLb")]
        cWO.inc(16)
        cQLp[0].inc(16)
        cQLp[1].inc(16)
        vQSa_t, vQSb_t = cQSp[0].n, cQSp[1].n
        cWO2.inc(16)
        def _f(sync, vQSa_t=vQSa_t, vQSb_t=vQSb_t):
            sync.wait_ge(cKVP.sem, 64)
            sync.wait_ge(cQSp[0].sem, vQSa_t)
            sync.wait_ge(cQSp[1].sem, vQSb_t)
            src = bass.AP(qs_h, 0, [[N, 128], [128 * N, 4], [1, 512]])
            sync.dma_start(
                q_sb[:, 0:2048].rearrange("p (c m) -> p c m", c=4),
                src).then_inc(cQLp[0].sem, 16)
            src = bass.AP(wout_h.tensor, 0, [[D, 128], [128 * D, 4], [1, D]])
            sync.dma_start(
                wout_sb[:, 0:4096].rearrange("p (c m) -> p c m", c=4),
                src).then_inc(cWO.sem, 16)
        SP(_f)
        def _f(scalar, vQSa_t=vQSa_t, vQSb_t=vQSb_t):
            scalar.wait_ge(cKVP.sem, 64)
            scalar.wait_ge(cQSp[0].sem, vQSa_t)
            scalar.wait_ge(cQSp[1].sem, vQSb_t)
            src = bass.AP(qs_h, 4 * 128 * N, [[N, 128], [128 * N, 4], [1, 512]])
            scalar.dma_start(
                q_sb[:, 2048:4096].rearrange("p (c m) -> p c m", c=4),
                src).then_inc(cQLp[1].sem, 16)
            src = bass.AP(wout_h.tensor, 4 * 128 * D,
                          [[D, 128], [128 * D, 4], [1, D]])
            scalar.dma_start(
                wout_sb[:, 4096:8192].rearrange("p (c m) -> p c m", c=4),
                src).then_inc(cWO2.sem, 16)
        SE(_f)
        barrier()

        qf_hosts = [poolB[:, (8 + 2 * t) * 1024:(8 + 2 * t) * 1024 + 2048]
                    for t in range(NT)]
        rb_sb = poolA[:, 14336:14848]
        sfm = poolC[0:16, 7680:8192]
        attnT = poolA[:, 12288:16384]
        xb2 = [poolC[:, 0:1024], poolC[:, 1024:2048]]
        attn_raw = poolC[:, 2048:6144]
        ln2dst = poolC[:, 2048:3072]
        wo_tmp = poolC[:, 6144:6656]
        q2b = [poolC[:, 6656:7168], poolC[:, 7168:7680]]
        eterm = poolC[0:16, 7168:7680]
        stq = poolA[:, 16384:16448]
        x1 = poolB

        cX2p = [C("cX2a"), C("cX2b")]
        cQ2 = C("cQ2"); cDQM = C("cDQM"); cQFe = C("cQFe"); cQFr = C("cQFr")
        cDQE = C("cDQE"); cUQT = C("cUQT"); cRMX = C("cRMX"); cSTT = C("cSTT")
        cSFM = C("cSFM"); cET = C("cET"); cUQ = C("cUQ")
        cQF = C("cQF"); cAM = C("cAM"); cAE = C("cAE"); cBM = C("cBM")
        cDR = C("cDR"); cRBM = C("cRBM"); cRBE = C("cRBE"); cSCL = C("cSCL")
        cWOM = C("cWOM"); cWOE = C("cWOE"); cWOT = C("cWOT"); cWOTE = C("cWOTE")
        cX1 = C("cX1"); cAP2 = C("cAP2")

        def emit_C1(tile):
            vQL = 16 * (tile + 1)
            for c in range(NC):
                vQ2 = cQ2.inc()
                def _f(scalar, tile=tile, c=c, vQL=vQL, vQ2=vQ2):
                    scalar.wait_ge(cQLp[c // 4].sem, vQL)
                    if c < 2 and tile > 0:
                        scalar.wait_ge(cDQM.sem, 8 * tile)
                    if c == 1 and tile > 0:
                        scalar.wait_ge(cRBM.sem, 8 * tile)
                    if vQ2 >= 3:
                        scalar.wait_ge(cDQM.sem, vQ2 - 2)
                    scalar.activation(q2b[c % 2][:],
                                      q_sb[:, c * 512:c * 512 + 512],
                                      AF.Square, bias=0.0, scale=1.0
                                      ).then_inc(cQ2.sem, 1)
                SE(_f)
                vDQM = cDQM.inc()
                def _f(tensor, tile=tile, c=c, vQ2=vQ2):
                    tensor.wait_ge(cQ2.sem, vQ2)
                    if c == 0 and tile > 0:
                        tensor.wait_ge(cDR.sem, tile)
                    tensor.matmul(pb[4][0:16, :], d2h[:, c * 16:c * 16 + 16],
                                  q2b[c % 2][:], start=(c == 0), stop=(c == 7)
                                  ).then_inc(cDQM.sem, 1)
                TE(_f)
            vDQE = cDQE.inc()
            def _f(scalar, tile=tile):
                scalar.wait_ge(cDQM.sem, tile * 8 + 8)
                if tile > 0:
                    scalar.wait_ge(cSFM.sem, tile)
                scalar.activation(sfm[:, :], pb[4][0:16, :], AF.Identity,
                                  bias=0.0, scale=0.5).then_inc(cDQE.sem, 1)
            SE(_f)
            for h in range(H):
                c = h // 2
                ro = (h % 2) * 64
                bank = pb[2 + h % 2]
                vUQT = cUQT.inc()
                def _f(tensor, h=h, c=c, ro=ro, bank=bank, vUQT=vUQT,
                       vQL=vQL, tile=tile):
                    tensor.wait_ge(cQLp[c // 4].sem, vQL)
                    if h == 0 and tile > 0:
                        tensor.wait_ge(cWOTE.sem, 8 * tile)
                    if vUQT >= 3:
                        tensor.wait_ge(cRMX.sem, vUQT - 2)
                    last = None
                    for j in range(4):
                        last = tensor.matmul(
                            bank[:, j * 128:j * 128 + 128],
                            q_sb[ro:ro + 64,
                                 c * 512 + j * 128:c * 512 + j * 128 + 128],
                            pm2[ro:ro + 64, h * 128:h * 128 + 128],
                            start=True, stop=True)
                    last.then_inc(cUQT.sem, 1)
                TE(_f)
                cRMX.inc()
                def _f(vector, h=h, bank=bank, vUQT=vUQT, tile=tile):
                    vector.wait_ge(cUQT.sem, vUQT)
                    if tile > 0 and h == 0:
                        vector.wait_ge(cSTT.sem, tile)
                    in_ap = bank[:].rearrange("p (j f) -> p j f", j=4)
                    out_ap = bass.AP(poolA, 16384 + h, [[16640, 128], [16, 4]])
                    vector.reduce_max(out_ap, in_ap, axis=AX
                                      ).then_inc(cRMX.sem, 1)
                VE(_f)
            vSTT = cSTT.inc()
            def _f(tensor, tile=tile, vDQE=vDQE):
                tensor.wait_ge(cRMX.sem, tile * 16 + 16)
                tensor.wait_ge(cDQE.sem, vDQE)
                last = None
                for j in range(4):
                    last = tensor.transpose(
                        pb[4][0:16, j * 128:j * 128 + 128],
                        stq[:, j * 16:j * 16 + 16], ident[:])
                last.then_inc(cSTT.sem, 1)
            TE(_f)
            vSFM = cSFM.inc()
            def _f(vector, vSTT=vSTT):
                vector.wait_ge(cSTT.sem, vSTT)
                vector.scalar_tensor_tensor(sfm[:, :], pb[4][0:16, :], 0.125,
                                            sfm[:, :], op0=OP.mult,
                                            op1=OP.add).then_inc(cSFM.sem, 1)
            VE(_f)
            vET = cET.inc()
            def _f(scalar, vSFM=vSFM, tile=tile):
                scalar.wait_ge(cSFM.sem, vSFM)
                if tile == 0:
                    scalar.wait_ge(cBP.sem, 1)
                else:
                    scalar.wait_ge(cRBM.sem, 8 * tile)
                scalar.activation(eterm[:, :], sfm[:, :], AF.Exp,
                                  bias=stk_t[:, 1:2], scale=1.0
                                  ).then_inc(cET.sem, 1)
            SE(_f)

        uq_v = {}
        ab_v = {}

        def emit_C2(tile, part="all"):
            def emit_AB_mm(pr):
                gp = tile * 8 + pr
                abank = pb[pr % 2]
                vAM = cAM.inc()
                ab_v[(tile, pr)] = vAM
                def _f(tensor, pr=pr, gp=gp, abank=abank, vAM=vAM, tile=tile):
                    tensor.wait_ge(cQFe.sem, gp * 4 + 4)
                    if pr < 2 and tile > 0:
                        tensor.wait_ge(cWOE.sem, 8 * tile)
                    if vAM >= 3:
                        tensor.wait_ge(cAE.sem, vAM - 2)
                    qf_sb = qf_hosts[tile]
                    last = None
                    for hh in range(2):
                        h = 2 * pr + hh
                        for half in range(2):
                            last = tensor.matmul(
                                abank[hh * 64:hh * 64 + 64, :],
                                kvA[:, h * 130 + half * 65:
                                    h * 130 + half * 65 + 64],
                                qf_sb[:, (2 * hh + half) * 512:
                                      (2 * hh + half) * 512 + 512],
                                start=(half == 0), stop=(half == 1))
                    last.then_inc(cAM.sem, 1)
                TE(_f)
                cBM.inc()
                def _f(tensor, pr=pr, tile=tile):
                    if pr == 0:
                        tensor.wait_ge(cSFM.sem, tile + 1)
                        if tile > 0:
                            tensor.wait_ge(cDR.sem, tile)
                    qf_sb = qf_hosts[tile]
                    last = None
                    for hh in range(2):
                        h = 2 * pr + hh
                        for half in range(2):
                            last = tensor.matmul(
                                pb[4][0:16, :],
                                bpad[:, (2 * h + half) * 16:
                                     (2 * h + half) * 16 + 16],
                                qf_sb[:, (2 * hh + half) * 512:
                                      (2 * hh + half) * 512 + 512],
                                start=(pr == 0 and hh == 0 and half == 0),
                                stop=(pr == 7 and hh == 1 and half == 1))
                    last.then_inc(cBM.sem, 1)
                TE(_f)
                return vAM

            def emit_AB_evac(pr, vAM=None):
                abank = pb[pr % 2]
                if vAM is None:
                    vAM = ab_v[(tile, pr)]
                vAE = cAE.inc()
                def _f(vector, pr=pr, abank=abank, vAM=vAM, tile=tile):
                    vector.wait_ge(cAM.sem, vAM)
                    if tile > 0:
                        vector.wait_ge(cWOM.sem, 8 * tile)
                    if pr == 0:
                        vector.drain()
                    vector.tensor_copy(attn_raw[:, pr * 512:pr * 512 + 512],
                                       abank[:]).then_inc(cAE.sem, 1)
                VE(_f)

            def emit_AB(pr):
                emit_AB_evac(pr, emit_AB_mm(pr))

            def emit_uq(pr, hh):
                h = 2 * pr + hh
                c = h // 2
                ro = (h % 2) * 64
                vUQ = cUQ.inc()
                bank = pb[(2, 3, 5, 6)[(vUQ - 1) % 4]]
                uq_v[(tile, pr, hh)] = (vUQ, bank)
                def _f(tensor, h=h, c=c, ro=ro, bank=bank, vUQ=vUQ,
                       pr=pr, hh=hh, tile=tile):
                    if pr == 0 and hh == 0:
                        tensor.wait_ge(cRMX.sem, 16 * (tile + 1))
                    if vUQ >= 5:
                        tensor.wait_ge(cQFe.sem, 2 * (vUQ - 4))
                    tensor.matmul(bank[:],
                                  pm2[ro:ro + 64, h * 128:h * 128 + 128],
                                  q_sb[ro:ro + 64, c * 512:c * 512 + 512],
                                  start=True, stop=True
                                  ).then_inc(cUQ.sem, 1)
                TE(_f)

            def emit_ex(pr, hh):
                gp = tile * 8 + pr
                vUQ, bank = uq_v[(tile, pr, hh)]
                cQFe.inc(2)
                def _f(scalar, hh=hh, bank=bank, vUQ=vUQ, gp=gp,
                       pr=pr, tile=tile):
                    scalar.wait_ge(cUQ.sem, vUQ)
                    if gp >= 1:
                        scalar.wait_ge(cAM.sem, gp)
                        scalar.wait_ge(cBM.sem, gp)
                    qf_sb = qf_hosts[tile]
                    for sgn in range(2):
                        scalar.activation(
                            qf_sb[:, (2 * hh + sgn) * 512:
                                  (2 * hh + sgn) * 512 + 512],
                            bank[:], AF.Exp, bias=0.0,
                            scale=(0.125 if sgn == 0 else -0.125)
                            ).then_inc(cQFe.sem, 1)
                SE(_f)

            if part == "head":
                for pr in (0, 1):
                    emit_uq(pr, 0); emit_ex(pr, 0)
                    emit_uq(pr, 1); emit_ex(pr, 1)
                emit_AB_mm(0)
                emit_uq(2, 0)
                emit_uq(2, 1)
                return
            if part == "rest":
                emit_ex(2, 0)
                emit_ex(2, 1)
                emit_AB_evac(0)
                emit_AB(1)
                pr_start = 3
            else:
                pr_start = 0
            for pr in range(pr_start, 8):
                for hh in range(2):
                    emit_uq(pr, hh)
                    emit_ex(pr, hh)
                if pr >= 1:
                    emit_AB(pr - 1)
            emit_AB(7)
            if tile + 1 < NT:
                cQLp[0].inc(16)
                cQLp[1].inc(16)
                def _f(sync, tile=tile):
                    sync.wait_ge(cQ2.sem, 8 * (tile + 1))
                    sync.wait_ge(cUQT.sem, 16 * (tile + 1))
                    sync.wait_ge(cUQ.sem, 16 * (tile + 1))
                    src = bass.AP(qs_h, (tile + 1) * 512,
                                  [[N, 128], [128 * N, 4], [1, 512]])
                    sync.dma_start(
                        q_sb[:, 0:2048].rearrange("p (c m) -> p c m", c=4),
                        src).then_inc(cQLp[0].sem, 16)
                    src = bass.AP(qs_h, 4 * 128 * N + (tile + 1) * 512,
                                  [[N, 128], [128 * N, 4], [1, 512]])
                    sync.dma_start(
                        q_sb[:, 2048:4096].rearrange("p (c m) -> p c m", c=4),
                        src).then_inc(cQLp[1].sem, 16)
                SP(_f)
            emit_x1_loads(tile)

        def emit_C3(tile):
            vDR = cDR.inc()
            def _f(vector, vDR=vDR, tile=tile):
                vector.wait_ge(cBM.sem, tile * 8 + 8)
                vector.wait_ge(cET.sem, tile + 1)
                vector.tensor_tensor(eterm[:, :], pb[4][0:16, :], eterm[:, :],
                                     op=OP.add)
                vector.drain()
                vector.reciprocal(eterm[:, :], eterm[:, :]
                                  ).then_inc(cDR.sem, 1)
            VE(_f)

            def emit_rbtail(pr):
                vRBE = cRBE.inc()
                rbank = pb[2 + pr % 2]
                def _f(scalar, rbank=rbank, pr=pr, tile=tile, vRBE=vRBE):
                    scalar.wait_ge(cRBM.sem, tile * 8 + pr + 1)
                    if vRBE >= 2:
                        scalar.wait_ge(cSCL.sem, vRBE - 1)
                    scalar.activation(rb_sb[:], rbank[:], AF.Identity,
                                      bias=0.0, scale=1.0).then_inc(cRBE.sem, 1)
                SE(_f)
                cSCL.inc()
                def _f(vector, pr=pr, vRBE=vRBE):
                    vector.wait_ge(cRBE.sem, vRBE)
                    vector.tensor_tensor(attn_raw[:, pr * 512:pr * 512 + 512],
                                         attn_raw[:, pr * 512:pr * 512 + 512],
                                         rb_sb[:], op=OP.mult
                                         ).then_inc(cSCL.sem, 1)
                VE(_f)

            for pr in range(8):
                rbank = pb[2 + pr % 2]
                vRBM = cRBM.inc()
                def _f(tensor, pr=pr, rbank=rbank, vDR=vDR, vRBM=vRBM,
                       tile=tile):
                    tensor.wait_ge(cDR.sem, vDR)
                    if pr == 0:
                        tensor.wait_ge(cQFe.sem, 32 * (tile + 1))
                    if vRBM >= 3:
                        tensor.wait_ge(cRBE.sem, vRBM - 2)
                    tensor.matmul(rbank[:], epair[:, pr * 128:pr * 128 + 128],
                                  eterm[:, :], start=True, stop=True
                                  ).then_inc(cRBM.sem, 1)
                TE(_f)
                if pr >= 1:
                    emit_rbtail(pr - 1)
            emit_rbtail(7)

        def emit_x1_loads(tile):
            for j in range(4):
                stt = tile * 4 + j
                cX2c = cX2p[j % 2]
                pre = cX2c.n if (tile > 0 or j >= 2) else None
                cX2c.inc(16)
                def _f(sync, stt=stt, cX2c=cX2c, pre=pre):
                    if pre is not None:
                        sync.wait_ge(cX2c.sem, pre)
                    if stt >= 8:
                        qt = (stt - 8) // 2
                        sync.wait_ge(cAM.sem, 8 * qt + 8)
                        sync.wait_ge(cBM.sem, 8 * qt + 8)
                    sync.dma_start(x1[:, stt * 1024:stt * 1024 + 1024],
                                   x_h[stt * 128:stt * 128 + 128, :]
                                   ).then_inc(cX2c.sem, 16)
                SP(_f)

        def emit_C4a(tile):
            for oc in range(NC):
                obank = pb[oc % 2]
                vWOM = cWOM.inc()
                def _f(tensor, oc=oc, obank=obank, vWOM=vWOM, tile=tile):
                    if oc == 0:
                        tensor.wait_ge(cWO.sem, 16)
                        tensor.wait_ge(cWO2.sem, 16)
                        tensor.wait_ge(cSCL.sem, tile * 8 + 8)
                    if vWOM >= 3:
                        tensor.wait_ge(cWOE.sem, vWOM - 2)
                    last = None
                    for ic in range(8):
                        last = tensor.matmul(
                            obank[:],
                            wout_sb[:, ic * 1024 + oc * 128:
                                    ic * 1024 + oc * 128 + 128],
                            attn_raw[:, ic * 512:ic * 512 + 512],
                            start=(ic == 0), stop=(ic == 7))
                    last.then_inc(cWOM.sem, 1)
                TE(_f)
                vWOE = cWOE.inc()
                def _f(vector, obank=obank, vWOM=vWOM, vWOE=vWOE):
                    vector.wait_ge(cWOM.sem, vWOM)
                    if vWOE >= 2:
                        vector.wait_ge(cWOT.sem, vWOE - 1)
                    vector.tensor_copy(wo_tmp[:], obank[:]).then_inc(cWOE.sem, 1)
                VE(_f)
                vWOT = cWOT.inc()
                def _f(tensor, oc=oc, vWOE=vWOE, vWOT=vWOT, tile=tile):
                    tensor.wait_ge(cWOE.sem, vWOE)
                    if oc == 0:
                        tensor.wait_ge(cRBE.sem, 8 * (tile + 1))
                    if vWOT >= 3:
                        tensor.wait_ge(cWOTE.sem, vWOT - 2)
                    b = pb[2 + oc % 2]
                    last = None
                    for j in range(4):
                        last = tensor.transpose(b[:, j * 128:j * 128 + 128],
                                                wo_tmp[:, j * 128:j * 128 + 128],
                                                ident[:])
                    last.then_inc(cWOT.sem, 1)
                TE(_f)
                cWOTE.inc()
                def _f(vector, oc=oc, vWOT=vWOT, tile=tile):
                    vector.wait_ge(cWOT.sem, vWOT)
                    if oc == 0 and tile > 0:
                        vector.wait_ge(cX1.sem, 4 * tile)
                    b = pb[2 + oc % 2]
                    last = None
                    for j in range(4):
                        last = vector.tensor_copy(
                            attnT[:, j * 1024 + oc * 128:
                                  j * 1024 + oc * 128 + 128],
                            b[:, j * 128:j * 128 + 128])
                    last.then_inc(cWOTE.sem, 1)
                VE(_f)

        def emit_C4b(tile):
            def emit_add(stt):
                j = stt % 4
                cX2c = cX2p[j % 2]
                cX1.inc()
                def _f(vector, stt=stt, j=j, cX2c=cX2c):
                    vector.wait_ge(cAP2.sem, stt + 1)
                    vector.wait_ge(cX2c.sem, 16 * (stt // 2 + 1))
                    xs = x1[:, stt * 1024:stt * 1024 + 1024]
                    vector.tensor_tensor(xs[:, 0:512], xs[:, 0:512],
                                         ln2dst[:, 0:512], op=OP.add)
                    vector.tensor_tensor(xs[:, 512:1024], xs[:, 512:1024],
                                         ln2dst[:, 512:1024],
                                         op=OP.add).then_inc(cX1.sem, 1)
                VE(_f)

            for j in range(4):
                stt = tile * 4 + j
                cX2c = cX2p[j % 2]
                def _f(vector, tile=tile):
                    vector.wait_ge(cWOTE.sem, tile * 8 + 8)
                VE(_f)
                cNBc, vNB, rsv, nbv = ln_stats_chain(
                    attnT[:, j * 1024:j * 1024 + 1024])
                cAP2.inc()
                def _f(scalar, j=j, cNBc=cNBc, vNB=vNB, rsv=rsv, nbv=nbv,
                       tile=tile, stt=stt):
                    scalar.wait_ge(cNBc.sem, vNB)
                    if j == 0:
                        scalar.wait_ge(cWOM.sem, 8 * (tile + 1))
                    if stt >= 1:
                        scalar.wait_ge(cX1.sem, stt)
                    scalar.activation(ln2dst[:, 0:512],
                                      attnT[:, j * 1024:j * 1024 + 512],
                                      AF.Identity, bias=nbv[:], scale=rsv[:])
                    scalar.activation(ln2dst[:, 512:1024],
                                      attnT[:, j * 1024 + 512:j * 1024 + 1024],
                                      AF.Identity, bias=nbv[:], scale=rsv[:]
                                      ).then_inc(cAP2.sem, 1)
                SE(_f)
                if j >= 1:
                    emit_add(stt - 1)
                if j == 3:
                    emit_add(stt)

        emit_C1(0)
        emit_C2(0)
        emit_C3(0)
        for tile in range(NT):
            emit_C4a(tile)
            if tile + 1 < NT:
                emit_C1(tile + 1)
                emit_C2(tile + 1, "head")
            emit_C4b(tile)
            if tile + 1 < NT:
                emit_C2(tile + 1, "rest")
                emit_C3(tile + 1)
        barrier()

        wffn_sb = poolA[:, 0:8192]
        h2_fm = poolA[:, 8192:12288]
        ffnT = poolA[:, 12288:16384]
        ut_sb = kvA[:, 0:1024]
        vfl_sb = kvA[:, 1024:2048]
        gwt_sb = bpad[:, 0:64]
        h2tb = [poolC[:, 0:1024], poolC[:, 1024:2048]]
        ytb = [poolC[:, 2048:3072], poolC[:, 3072:4096]]
        ffn_tmp = poolC[:, 4096:4608]
        w_sb = poolC[:, 4608:5120]
        gateb_sb = poolC[:, 5120:5632]
        ln4a = [poolC[:, 5632:6144], poolC[:, 6144:6656]]
        glog = poolC[0:8, 6656:7168]
        gfm = poolC[0:8, 7168:7680]

        cWF = C("cWF")
        cWF.inc(64)
        def _f(sync):
            src = bass.AP(gatewT_h.tensor, 0, [[8, 128], [128 * 8, 8], [1, 8]])
            sync.dma_start(gwt_sb[:].rearrange("p (c m) -> p c m", c=8),
                           src).then_inc(cWF.sem, 16)
            sync.wait_ge(cWF.sem, 16)
            src = bass.AP(ut_h.tensor, 0, [[128, 128], [128 * 128, 8], [1, 128]])
            sync.dma_start(ut_sb[:].rearrange("p (c m) -> p c m", c=8),
                           src).then_inc(cWF.sem, 16)
            sync.wait_ge(cWF.sem, 32)
            src = bass.AP(wffnT_h.tensor, 0, [[D, 128], [128 * D, 8], [1, D]])
            sync.dma_start(wffn_sb[:].rearrange("p (c m) -> p c m", c=8),
                           src).then_inc(cWF.sem, 16)
            sync.wait_ge(cWF.sem, 48)
            sync.dma_start(vfl_sb[:], vflat_h).then_inc(cWF.sem, 16)
        SP(_f)

        cLN3 = C("cLN3"); cH2T = C("cH2T"); cH2E = C("cH2E")
        cGM = C("cGM"); cGE = C("cGE"); cGT = C("cGT"); cSMX = C("cSMX")
        cGT2 = C("cGT2"); cGF = C("cGF"); cRGM = C("cRGM"); cRGE = C("cRGE")
        cTM = C("cTM"); cWM = C("cWM"); cFFM = C("cFFM"); cFFE = C("cFFE")
        cFFT = C("cFFT"); cFFTE = C("cFFTE"); cLN4 = C("cLN4"); cY = C("cY")
        cYSp = [C("cYSa"), C("cYSb")]

        def emit_D1(tile):
            def emit_h2evac(j):
                cH2E.inc()
                def _f(vector, j=j, tile=tile):
                    vector.wait_ge(cH2T.sem, tile * 4 + j + 1)
                    if j == 0 and tile > 0:
                        vector.wait_ge(cFFM.sem, 8 * tile)
                        vector.wait_ge(cGM.sem, tile)
                        vector.wait_ge(cTM.sem, tile)
                    pbase = 2 + (j % 2)
                    last = None
                    for c in range(NC):
                        b = pb[2 + c // 4]
                        last = vector.tensor_copy(
                            h2_fm[:, c * 512 + j * 128:c * 512 + j * 128 + 128],
                            b[:, (c % 4) * 128:(c % 4) * 128 + 128])
                    last.then_inc(cH2E.sem, 1)
                VE(_f)

            for j in range(4):
                stt = tile * 4 + j
                cNBc, vNB, rsv, nbv = ln_stats_chain(
                    x1[:, stt * 1024:stt * 1024 + 1024])
                vLN3 = cLN3.inc()
                def _f(scalar, stt=stt, j=j, cNBc=cNBc, vNB=vNB, rsv=rsv,
                       nbv=nbv, vLN3=vLN3):
                    scalar.wait_ge(cNBc.sem, vNB)
                    if vLN3 >= 3:
                        scalar.wait_ge(cH2T.sem, vLN3 - 2)
                    scalar.activation(h2tb[j % 2][:],
                                      x1[:, stt * 1024:stt * 1024 + 1024],
                                      AF.Identity, bias=nbv[:], scale=rsv[:]
                                      ).then_inc(cLN3.sem, 1)
                SE(_f)
                vH2T = cH2T.inc()
                def _f(tensor, j=j, vLN3=vLN3, vH2T=vH2T, tile=tile):
                    tensor.wait_ge(cLN3.sem, vLN3)
                    if j == 0 and tile > 0:
                        tensor.wait_ge(cFFTE.sem, 8 * tile)
                        tensor.wait_ge(cSMX.sem, 6 * tile)
                    if vH2T >= 2:
                        tensor.wait_ge(cH2E.sem, vH2T - 1)
                    last = None
                    for c in range(NC):
                        b = pb[2 + c // 4]
                        last = tensor.transpose(
                            b[:, (c % 4) * 128:(c % 4) * 128 + 128],
                            h2tb[j % 2][:, c * 128:c * 128 + 128], ident[:])
                    last.then_inc(cH2T.sem, 1)
                TE(_f)
                if j >= 1:
                    emit_h2evac(j - 1)
                if j == 3:
                    emit_h2evac(3)

        def emit_D2(tile):
            vGM = cGM.inc()
            def _f(tensor, tile=tile, vGM=vGM):
                tensor.wait_ge(cWF.sem, 16)
                tensor.wait_ge(cH2E.sem, 4 * (tile + 1))
                if tile > 0:
                    tensor.wait_ge(cGF.sem, tile)
                last = None
                for ic in range(8):
                    last = tensor.matmul(pb[4][0:8, :],
                                         gwt_sb[:, ic * 8:ic * 8 + 8],
                                         h2_fm[:, ic * 512:ic * 512 + 512],
                                         start=(ic == 0), stop=(ic == 7))
                last.then_inc(cGM.sem, 1)
            TE(_f)
            vGE = cGE.inc()
            def _f(scalar, vGM=vGM, tile=tile):
                scalar.wait_ge(cGM.sem, vGM)
                if tile > 0:
                    scalar.wait_ge(cGT.sem, tile)
                scalar.activation(glog[:, :], pb[4][0:8, :], AF.Identity,
                                  bias=0.0, scale=1.0).then_inc(cGE.sem, 1)
            SE(_f)
            vGT = cGT.inc()
            def _f(tensor, vGE=vGE):
                tensor.wait_ge(cGE.sem, vGE)
                last = None
                for j in range(4):
                    last = tensor.transpose(pb[2][:, j * 8:j * 8 + 8],
                                            glog[:, j * 128:j * 128 + 128],
                                            ident[0:8, 0:8])
                last.then_inc(cGT.sem, 1)
            TE(_f)
            vS1 = cSMX.inc()
            def _f(vector, vGT=vGT, tile=tile):
                vector.wait_ge(cGT.sem, vGT)
                if tile > 0:
                    vector.wait_ge(cGT2.sem, tile)
                in_ap = pb[2][:, 0:32].rearrange("p (j f) -> p j f", j=4)
                out_ap = gneg[:].rearrange("p (j o) -> p j o", o=1)
                vector.reduce_max(out_ap, in_ap, axis=AX, negate=True
                                  ).then_inc(cSMX.sem, 1)
            VE(_f)
            for j in range(4):
                cSMX.inc()
                def _f(scalar, j=j, vS1=vS1):
                    scalar.wait_ge(cSMX.sem, vS1)
                    scalar.activation(gtmp[:, j * 8:j * 8 + 8],
                                      pb[2][:, j * 8:j * 8 + 8], AF.Exp,
                                      bias=gneg[:, j:j + 1], scale=1.0
                                      ).then_inc(cSMX.sem, 1)
                SE(_f)
            cSMX.inc()
            def _f(vector, vS1=vS1, tile=tile):
                vector.wait_ge(cSMX.sem, vS1 + 4)
                in_ap = gtmp[:].rearrange("p (j f) -> p j f", j=4)
                out_ap = gsum[:].rearrange("p (j o) -> p j o", o=1)
                vector.reduce_sum(out_ap, in_ap, axis=AX)
                vector.drain()
                vector.reciprocal(gsum[:], gsum[:])
                vector.drain()
                gb = bass.AP(gsum, 0, [[4, 128], [1, 4], [0, 8]])
                vector.tensor_tensor(gtmp[:].rearrange("p (j f) -> p j f", j=4),
                                     gtmp[:].rearrange("p (j f) -> p j f", j=4),
                                     gb, op=OP.mult).then_inc(cSMX.sem, 1)
            VE(_f)
            vGT2 = cGT2.inc()
            def _f(tensor, tile=tile, vGT2=vGT2):
                tensor.wait_ge(cSMX.sem, tile * 6 + 6)
                last = None
                for j in range(4):
                    last = tensor.transpose(pb[4][0:8, j * 128:j * 128 + 128],
                                            gtmp[:, j * 8:j * 8 + 8],
                                            ident[:])
                last.then_inc(cGT2.sem, 1)
            TE(_f)
            vGF = cGF.inc()
            def _f(scalar, vGT2=vGT2, tile=tile):
                scalar.wait_ge(cGT2.sem, vGT2)
                if tile > 0:
                    scalar.wait_ge(cRGM.sem, tile)
                scalar.activation(gfm[:, :], pb[4][0:8, :], AF.Identity,
                                  bias=0.0, scale=1.0).then_inc(cGF.sem, 1)
            SE(_f)
            vRGM = cRGM.inc()
            def _f(tensor, vGF=vGF, tile=tile):
                tensor.wait_ge(cGF.sem, vGF)
                if tile > 0:
                    tensor.wait_ge(cRGE.sem, tile)
                tensor.matmul(pb[5][:], rgate[:], gfm[:, :], start=True,
                              stop=True).then_inc(cRGM.sem, 1)
            TE(_f)
            vRGE = cRGE.inc()
            def _f(scalar, vRGM=vRGM, tile=tile):
                scalar.wait_ge(cRGM.sem, vRGM)
                if tile > 0:
                    scalar.wait_ge(cWM.sem, tile)
                scalar.activation(gateb_sb[:], pb[5][:], AF.Identity,
                                  bias=0.0, scale=1.0).then_inc(cRGE.sem, 1)
            SE(_f)
            vTM = cTM.inc()
            def _f(tensor, vTM=vTM):
                if vTM == 1:
                    tensor.wait_ge(cWF.sem, 32)
                if vTM >= 2:
                    tensor.wait_ge(cWM.sem, vTM - 1)
                last = None
                for ic in range(8):
                    last = tensor.matmul(pb[6][:],
                                         ut_sb[:, ic * 128:ic * 128 + 128],
                                         h2_fm[:, ic * 512:ic * 512 + 512],
                                         start=(ic == 0), stop=(ic == 7))
                last.then_inc(cTM.sem, 1)
            TE(_f)
            vWM = cWM.inc()
            def _f(vector, vTM=vTM, vRGE=vRGE, tile=tile):
                vector.wait_ge(cTM.sem, vTM)
                vector.wait_ge(cRGE.sem, vRGE)
                if tile > 0:
                    vector.wait_ge(cFFM.sem, 8 * tile)
                vector.tensor_tensor(w_sb[:], pb[6][:], gateb_sb[:],
                                     op=OP.mult).then_inc(cWM.sem, 1)
            VE(_f)

        def emit_D3(tile):
            for pc in range(4):
                oc0, oc1 = 2 * pc, 2 * pc + 1
                g0 = tile * 8 + oc0
                cFFM.inc(2)
                def _f(tensor, pc=pc, oc0=oc0, oc1=oc1, g0=g0, tile=tile):
                    if pc == 0:
                        tensor.wait_ge(cH2E.sem, 4 * (tile + 1))
                        if tile == 0:
                            tensor.wait_ge(cWF.sem, 48)
                    if g0 >= 2:
                        tensor.wait_ge(cFFE.sem, g0)
                    for oc, bk in ((oc0, pb[0]), (oc1, pb[1])):
                        for ic in range(8):
                            tensor.matmul(
                                bk[:],
                                wffn_sb[:, ic * 1024 + oc * 128:
                                        ic * 1024 + oc * 128 + 128],
                                h2_fm[:, ic * 512:ic * 512 + 512],
                                start=(ic == 0), stop=False)
                    tensor.wait_ge(cWM.sem, tile + 1)
                    if tile == 0 and pc == 0:
                        tensor.wait_ge(cWF.sem, 64)
                    tensor.matmul(pb[0][:], vfl_sb[:, oc0 * 128:oc0 * 128 + 128],
                                  w_sb[:], start=False, stop=True
                                  ).then_inc(cFFM.sem, 1)
                    tensor.matmul(pb[1][:], vfl_sb[:, oc1 * 128:oc1 * 128 + 128],
                                  w_sb[:], start=False, stop=True
                                  ).then_inc(cFFM.sem, 1)
                TE(_f)
                for oc, bki in ((oc0, 0), (oc1, 1)):
                    g = tile * 8 + oc
                    vFFE = cFFE.inc()
                    def _f(vector, bki=bki, g=g, vFFE=vFFE):
                        vector.wait_ge(cFFM.sem, g + 1)
                        if vFFE >= 2:
                            vector.wait_ge(cFFT.sem, vFFE - 1)
                        vector.tensor_copy(ffn_tmp[:], pb[bki][:]
                                           ).then_inc(cFFE.sem, 1)
                    VE(_f)
                    vFFT = cFFT.inc()
                    def _f(tensor, oc=oc, vFFE=vFFE, vFFT=vFFT):
                        tensor.wait_ge(cFFE.sem, vFFE)
                        if vFFT >= 3:
                            tensor.wait_ge(cFFTE.sem, vFFT - 2)
                        b = pb[2 + oc % 2]
                        last = None
                        for j in range(4):
                            last = tensor.transpose(
                                b[:, j * 128:j * 128 + 128],
                                ffn_tmp[:, j * 128:j * 128 + 128], ident[:])
                        last.then_inc(cFFT.sem, 1)
                    TE(_f)
                    cFFTE.inc()
                    def _f(vector, oc=oc, vFFT=vFFT, tile=tile):
                        vector.wait_ge(cFFT.sem, vFFT)
                        if oc == 0 and tile > 0:
                            vector.wait_ge(cLN4.sem, 4 * tile)
                        b = pb[2 + oc % 2]
                        last = None
                        for j in range(4):
                            last = vector.tensor_copy(
                                ffnT[:, j * 1024 + oc * 128:
                                     j * 1024 + oc * 128 + 128],
                                b[:, j * 128:j * 128 + 128])
                        last.then_inc(cFFTE.sem, 1)
                    VE(_f)

        def emit_D4(tile):
            def emit_yout(stt):
                vY = cY.inc()
                cYSc = cYSp[stt % 2]
                def _f(vector, stt=stt, vY=vY, cYSc=cYSc):
                    vector.wait_ge(cLN4.sem, stt + 1)
                    if stt >= 2:
                        vector.wait_ge(cYSc.sem, 16 * (stt // 2))
                    vector.tensor_tensor(ytb[stt % 2][:, 0:512],
                                         x1[:, stt * 1024:stt * 1024 + 512],
                                         ln4a[0][:], op=OP.add)
                    vector.tensor_tensor(ytb[stt % 2][:, 512:1024],
                                         x1[:, stt * 1024 + 512:
                                            stt * 1024 + 1024],
                                         ln4a[1][:], op=OP.add
                                         ).then_inc(cY.sem, 1)
                VE(_f)
                cYSc.inc(16)
                def _f(gpsimd, stt=stt, vY=vY, cYSc=cYSc):
                    gpsimd.wait_ge(cY.sem, vY)
                    gpsimd.dma_start(y_h[stt * 128:stt * 128 + 128, :],
                                     ytb[stt % 2][:]).then_inc(cYSc.sem, 16)
                GP(_f)

            for j in range(4):
                stt = tile * 4 + j
                def _f(vector, tile=tile):
                    vector.wait_ge(cFFTE.sem, tile * 8 + 8)
                VE(_f)
                cNBc, vNB, rsv, nbv = ln_stats_chain(
                    ffnT[:, j * 1024:j * 1024 + 1024])
                vLN4 = cLN4.inc()
                def _f(scalar, j=j, cNBc=cNBc, vNB=vNB, rsv=rsv, nbv=nbv,
                       stt=stt):
                    scalar.wait_ge(cNBc.sem, vNB)
                    if stt >= 1:
                        scalar.wait_ge(cY.sem, stt)
                    scalar.activation(ln4a[0][:],
                                      ffnT[:, j * 1024:j * 1024 + 512],
                                      AF.Identity, bias=nbv[:], scale=rsv[:])
                    scalar.activation(ln4a[1][:],
                                      ffnT[:, j * 1024 + 512:j * 1024 + 1024],
                                      AF.Identity, bias=nbv[:], scale=rsv[:]
                                      ).then_inc(cLN4.sem, 1)
                SE(_f)
                if j >= 1:
                    emit_yout(stt - 1)
                if j == 3:
                    emit_yout(stt)

        emit_D1(0)
        emit_D2(0)
        for tile in range(NT):
            emit_D3(tile)
            if tile + 1 < NT:
                emit_D1(tile + 1)
                emit_D2(tile + 1)
            emit_D4(tile)

        vYSa_f, vYSb_f = cYSp[0].n, cYSp[1].n
        def _f(gpsimd, vYSa_f=vYSa_f, vYSb_f=vYSb_f):
            gpsimd.wait_ge(cYSp[0].sem, vYSa_f)
            gpsimd.wait_ge(cYSp[1].sem, vYSb_f)
        GP(_f)
        def _f(sync, vYSa_f=vYSa_f, vYSb_f=vYSb_f):
            sync.wait_ge(cYSp[0].sem, vYSa_f)
            sync.wait_ge(cYSp[1].sem, vYSb_f)
        SP(_f)

        with nc.Block() as block:
            def run_for(ek):
                def body(eng):
                    for e, fn in plan:
                        if e == ek:
                            fn(eng)
                return body
            block.sync(run_for("sync"))
            block.tensor(run_for("tensor"))
            block.vector(run_for("vector"))
            block.scalar(run_for("scalar"))
            block.gpsimd(run_for("gpsimd"))

    return nc


def _host_prep(L0_attn_x, L0_attn_W_qkv, L0_attn_W_out, L0_attn_proj,
               L0_ffn_weight, L0_ffn_gate_w, L0_ffn_lora_U, L0_ffn_lora_V):
    pmT = np.transpose(np.asarray(L0_attn_proj), (0, 2, 1))
    pm2 = np.concatenate([pmT, pmT], axis=1)
    pm2 = np.ascontiguousarray(
        np.transpose(pm2, (1, 0, 2)).reshape(128, H * 128))
    shared = {
        "wqkv": np.ascontiguousarray(np.asarray(L0_attn_W_qkv, np.float32)),
        "wout": np.ascontiguousarray(np.asarray(L0_attn_W_out, np.float32)),
        "pm2": pm2.astype(np.float32),
        "wffnT": np.ascontiguousarray(np.asarray(L0_ffn_weight, np.float32).T),
        "gatewT": np.ascontiguousarray(np.asarray(L0_ffn_gate_w, np.float32).T),
        "ut": np.ascontiguousarray(
            np.asarray(L0_ffn_lora_U, np.float32).reshape(128, D).T),
        "vflat": np.ascontiguousarray(
            np.transpose(np.asarray(L0_ffn_lora_V, np.float32),
                         (0, 2, 1)).reshape(128, D)),
    }
    x = np.asarray(L0_attn_x, np.float32)
    return [dict(shared, x=np.ascontiguousarray(x[b])) for b in range(NCORES)]


def kernel(**inputs):
    in_maps = _host_prep(**inputs)
    nc = build_nc()
    res = run_bass_kernel_spmd(nc, in_maps, core_ids=list(range(NCORES)))
    return np.stack([np.asarray(r["y"], dtype=np.float32)
                     for r in res.results], axis=0)
